# version 01 (final)
# baseline (speedup 1.0000x reference)
import contextlib
import numpy as np
import ml_dtypes
import concourse.bass as bass
import concourse.mybir as mybir
from concourse.bass_utils import run_bass_kernel_spmd

F32 = mybir.dt.float32
BF16 = mybir.dt.bfloat16
AF = mybir.ActivationFunctionType
OP = mybir.AluOpType
AX = mybir.AxisListType

B, S, D = 4, 8192, 1024
H, HD, KCONV = 4, 256, 4
SH = S // 2
NT = SH // 128
NCHUNK = SH // 512
C = 128
E0, F0 = 8, 4096
E1, F1, FS = 16, 512, 1024
G1, H1 = 8, 128
LN_EPS = 1e-5
L2_EPS = 1e-6
RMS_EPS = 1e-6
N_CORES = 8
ENGINES = ("t", "v", "s", "g", "y")


def f32c(x):
    return np.ascontiguousarray(np.asarray(x, dtype=np.float32))


class SemC:
    def __init__(self, handle, producer):
        self.h = handle
        self.producer = producer
        self.n = 0

    def bump(self, k):
        self.n += k
        return (self, self.n)


class Sched:

    def __init__(self, nc, stack):
        self.nc = nc
        self.stack = stack
        self.ops = {e: [] for e in ENGINES}
        self.waited = {e: {} for e in ENGINES}
        self.ev_meta = {}
        self._semi = 0

    def sem(self, producer):
        h = self.stack.enter_context(self.nc.semaphore(f"sm{self._semi}"))
        self._semi += 1
        return SemC(h, producer)

    def op(self, eng, fn, waits=(), sem=None, k=1):
        wl = []
        need_nop = False
        for w in waits:
            if w is None:
                continue
            sc, val = w
            cur = self.waited[eng].get(sc, 0)
            if val > cur:
                self.waited[eng][sc] = val
                wl.append((sc.h, val))
        ev = sem.bump(k) if sem is not None else None
        if ev is not None:
            self.ev_meta[(id(sem), ev[1])] = len(self.ops[eng])
        self.ops[eng].append((wl, fn, sem.h if sem is not None else None, k, False))
        return ev

    def dma(self, eng, out, in_, waits=(), sem=None, **kw):
        q = getattr(sem, "dma_queue", None)
        if q is None:
            sem.dma_queue = eng
        else:
            assert q == eng, f"DMA sem used from two queues: {q} vs {eng}"
        fn = (lambda o, i, kw2: (lambda e: e.dma_start(out=o, in_=i, **kw2)))(out, in_, kw)
        ev = self.op(eng, fn, waits=waits, sem=sem, k=16)
        self.ops[eng][-1] = self.ops[eng][-1][:4] + (True,)
        return ev

    def finalize(self, blk):
        def make(eng_name):
            def body(e):
                for wl, fn, semh, k, is_dma in self.ops[eng_name]:
                    for h, v in wl:
                        e.wait_ge(h, v)
                    ins = fn(e)
                    if semh is not None:
                        ins.then_inc(semh, k)
            return body
        blk.tensor(make("t"))
        blk.vector(make("v"))
        blk.scalar(make("s"))
        blk.gpsimd(make("g"))
        blk.sync(make("y"))


def emit_identity_ops(sch, eng, ident_ap, sem):
    n = ident_ap.shape[-1]
    ev = sch.op(eng, (lambda a: lambda g: g.memset(a, 0.0))(ident_ap), sem=sem)
    return sch.op(eng, (lambda a, n: lambda g: g.affine_select(
        out=a, in_=a, compare_op=OP.not_equal, fill=1.0,
        base=0, pattern=[[-1, n]], channel_multiplier=1))(ident_ap, n),
        waits=[ev], sem=sem)


def emit_tri_ops(sch, eng, ap, kind, sem):
    n = ap.shape[-1]
    cmp, cm, stp = {"lower_strict": (OP.is_gt, 1, -1),
                    "lower_incl": (OP.is_ge, 1, -1),
                    "upper_strict": (OP.is_gt, -1, 1),
                    "upper_incl": (OP.is_ge, -1, 1)}[kind]
    ev = sch.op(eng, (lambda a: lambda g: g.memset(a, 1.0))(ap), sem=sem)
    return sch.op(eng, (lambda a, n, cmp, cm, stp: lambda g: g.affine_select(
        out=a, in_=a, compare_op=cmp, fill=0.0,
        base=0, pattern=[[stp, n]], channel_multiplier=cm))(ap, n, cmp, cm, stp),
        waits=[ev], sem=sem)



def build_pre(dbg=False):
    nc = bass.Bass("TRN2", target_bir_lowering=False, debug=False)
    xs = nc.dram_tensor("xs", [SH, D], F32, kind="ExternalInput").ap()
    xhalo = nc.dram_tensor("xhalo", [3, D], F32, kind="ExternalInput").ap()
    W_in = nc.dram_tensor("W_in", [D, 3 * D], F32, kind="ExternalInput").ap()
    W_ctrl = nc.dram_tensor("W_ctrl", [D, 1280], F32, kind="ExternalInput").ap()
    b_ctrlT = nc.dram_tensor("b_ctrlT", [128, 10], F32, kind="ExternalInput").ap()
    cwall = nc.dram_tensor("cwall", [128, 24, 4], F32, kind="ExternalInput").ap()
    rampT = nc.dram_tensor("rampT", [128, 8], F32, kind="ExternalInput").ap()

    qhT = nc.dram_tensor("qhT", [D, SH], BF16, kind="ExternalOutput").ap()
    khT = nc.dram_tensor("khT", [D, SH], BF16, kind="ExternalOutput").ap()
    Vout = nc.dram_tensor("V", [SH, D], BF16, kind="ExternalOutput").ap()
    ogT = nc.dram_tensor("ogT", [D, SH], BF16, kind="ExternalOutput").ap()
    alpha = nc.dram_tensor("alpha", [H, SH], F32, kind="ExternalOutput").ap()
    beta = nc.dram_tensor("beta", [H, SH], F32, kind="ExternalOutput").ap()
    if dbg:
        dbg_proj = nc.dram_tensor("dbg_proj", [128, 8, 515], BF16, kind="ExternalOutput").ap()
        dbg_conv = nc.dram_tensor("dbg_conv", [128, 8, 512], BF16, kind="ExternalOutput").ap()
        dbg_si = nc.dram_tensor("dbg_si", [128, 8, 512], BF16, kind="ExternalOutput").ap()
        dbg_xT = nc.dram_tensor("dbg_xT", [128, 8, 640], BF16, kind="ExternalOutput").ap()
        dbg_W = nc.dram_tensor("dbg_W", [128, 8, 8, 128], BF16, kind="ExternalOutput").ap()

    W_in_r = W_in.rearrange("(dj p) (fj q) -> dj p fj q", p=128, q=128)
    W_ct_r = W_ctrl.rearrange("(dj p) (fj q) -> dj p fj q", p=128, q=128)

    with contextlib.ExitStack() as stack:
        sb = lambda *a: stack.enter_context(nc.sbuf_tensor(*a))
        psb = lambda *a: stack.enter_context(nc.psum_tensor(*a))
        ident = sb("ident", [128, 128], BF16)
        xT = sb("xT", [128, 8, SH + 3], BF16)
        Wsb = sb("Wsb", [128, 2, 8, 8, 128], BF16)
        xtile = sb("xtile", [128, 2, D], BF16)
        xhalo_sb = sb("xhalo_sb", [3, D], BF16)
        proj = sb("proj", [128, 2, 8, 515], BF16)
        convb = sb("convb", [128, 2, 8, 512], BF16)
        convsi = sb("convsi", [128, 2, 8, 512], BF16)
        cw = sb("cw", [128, 24, 4], F32)
        qknorm = sb("qknorm", [128, 2, 8, 512], BF16)
        sqb = sb("sqb", [128, 2, 512], F32)
        riprow = sb("riprow", [1, 2, 512], F32)
        vtokb = sb("vtokb", [128, 2, 4, D], BF16)
        ogb = sb("ogb", [128, 2, 512], F32)
        onescol = sb("onescol", [128, 1], F32)
        onesrow = sb("onesrow", [1, 128], F32)
        ramps = sb("ramps", [128, 8], F32)
        bctl = sb("bctl", [128, 10], F32)
        abf = sb("abf", [4, 4, 512], F32)

        ps_mm = psb("ps_mm", [128, 3, 512], F32)
        ps_tr = psb("ps_tr", [128, 1024], BF16)
        ps_bc = psb("ps_bc", [128, 2, 512], F32)
        ps_row = psb("ps_row", [1, 2, 512], F32)

        sch = Sched(nc, stack)
        CONST = sch.sem("dma")
        XLD = [sch.sem("dma"), sch.sem("dma")]
        WL = [sch.sem("dma"), sch.sem("dma")]
        WAB = sch.sem("dma")
        XT_ = sch.sem("t")
        XE = sch.sem("v")
        MM = sch.sem("t")
        EV = sch.sem("s")
        CV = sch.sem("v")
        SI = sch.sem("s")
        NM = sch.sem("v")
        RS = sch.sem("s")
        TB = sch.sem("t")
        QST = [sch.sem("dma"), sch.sem("dma")]
        VST = [sch.sem("dma"), sch.sem("dma")]
        OGST = [sch.sem("dma"), sch.sem("dma")]
        ABST = [sch.sem("dma") for _ in range(4)]
        DBGST = sch.sem("dma")
        GI = sch.sem("g")

        ev_id = emit_identity_ops(sch, "g", ident[:, :], GI)
        ev_ones = sch.op("g", lambda g: g.memset(onescol[:, :], 1.0), sem=GI)
        ev_ones2 = sch.op("g", lambda g: g.memset(onesrow[:, :], 1.0), sem=GI)
        sch.dma("g", xhalo_sb[:, :], xhalo, sem=CONST)
        sch.dma("g", cw[:, :, :], cwall, sem=CONST)
        sch.dma("g", ramps[:, :], rampT, sem=CONST)
        ev_bctl = sch.dma("g", bctl[:, :], b_ctrlT, sem=CONST)
        ev_halo = ev_cw = ev_bctl
        ev_ramp = sch.op("s", lambda s: s.activation(
            ramps[:, :], ramps[:, :], AF.Sigmoid), waits=[ev_bctl], sem=RS)

        ev_xe = [None] * (NT + 1)
        ev_tr0 = None
        for dj in range(8):
            ev_tr0 = sch.op("t", (lambda dj: (lambda t: t.transpose(
                ps_tr[:, dj * 128:dj * 128 + 3],
                xhalo_sb[:, dj * 128:(dj + 1) * 128], ident[0:3, 0:3])))(dj),
                waits=[ev_halo, ev_id], sem=XT_)
        ev_xe[0] = sch.op("v", lambda v: v.tensor_copy(
            xT[:, :, 0:3],
            ps_tr.rearrange("p (dj c) -> p dj c", c=128)[:, :, 0:3]),
            waits=[ev_tr0], sem=XE)
        ev_xload = [None] * NT
        for i in range(min(2, NT)):
            ev_xload[i] = sch.dma("g", xtile[:, i % 2, :], xs[i * 128:(i + 1) * 128, :],
                                  waits=[], sem=XLD[i % 2])
        for i in range(NT):
            ev_t = None
            for dj in range(8):
                ev_t = sch.op("t", (lambda b2, dj: (lambda t: t.transpose(
                    ps_tr[:, dj * 128:(dj + 1) * 128],
                    xtile[:, b2, dj * 128:(dj + 1) * 128], ident[:, :])))(i % 2, dj),
                    waits=[ev_xload[i], ev_xe[i]], sem=XT_)
            ev_xe[i + 1] = sch.op("v", (lambda i: (lambda v: v.tensor_copy(
                xT[:, :, 3 + i * 128: 3 + (i + 1) * 128],
                ps_tr.rearrange("p (dj c) -> p dj c", c=128))))(i),
                waits=[ev_t], sem=XE)
            if i + 2 < NT:
                ev_xload[i + 2] = sch.dma("g", xtile[:, i % 2, :],
                                          xs[(i + 2) * 128:(i + 3) * 128, :],
                                          waits=[ev_xe[i + 1]], sem=XLD[i % 2])
        ev_xT_all = ev_xe[NT]

        def load_wpart(part, wait_ev):
            src = W_in_r if part < 3 else W_ct_r
            fj0 = part * 8 if part < 3 else 0
            nfj = 8
            evw = None
            for dj in range(8):
                evw = sch.dma("g", Wsb[:, part % 2, dj, 0:nfj, :],
                              src[dj, :, fj0:fj0 + nfj, :],
                              waits=[wait_ev], sem=WL[part % 2])
            if part == 3:
                pass
            return evw

        def load_ctrl_ab(wait_ev):
            evw = None
            for dj in range(8):
                evw = sch.dma("g", Wsb[:, 0, dj, 0:2, :],
                              W_ct_r[dj, :, 8:10, :], waits=[wait_ev], sem=WAB)
            return evw

        ev_w = {0: load_wpart(0, None), 1: load_wpart(1, None)}

        mm_evac = {}
        last_mm_of_part = {}
        ev_silu = {}
        last_conv_ev = None
        ev_consumed = {}
        prev_m1 = {}
        prev_sr = {}
        prev_done = {}
        ev_qkstore = {}
        ev_vst = {}
        ev_ogstore = {}
        ev_abstore = {}
        prev_og2 = None
        prev_vte = None
        mmctr = 0

        def store_qk(part, tc, dst_dram, ev_ready):
            sl = tc % 2
            dones = []
            for hh in range(H):
                t0, t1 = 2 * hh, 2 * hh + 1
                rb = hh % 2
                sq0 = sch.op("v", (lambda sl, t0, rb: (lambda v: v.tensor_mul(
                    sqb[:, rb, :], convsi[:, sl, t0, :], convsi[:, sl, t0, :])))(sl, t0, rb),
                    waits=[ev_ready[t0], prev_m1.get(rb)], sem=NM)
                m0 = sch.op("t", (lambda rb: (lambda t: t.matmul(
                    ps_row[:, rb, :], onescol[:, :], sqb[:, rb, :],
                    start=True, stop=False)))(rb),
                    waits=[sq0, ev_ones, prev_sr.get(rb)], sem=TB)
                sq1 = sch.op("v", (lambda sl, t1, rb: (lambda v: v.tensor_mul(
                    sqb[:, rb, :], convsi[:, sl, t1, :], convsi[:, sl, t1, :])))(sl, t1, rb),
                    waits=[ev_ready[t1], m0], sem=NM)
                m1 = sch.op("t", (lambda rb: (lambda t: t.matmul(
                    ps_row[:, rb, :], onescol[:, :], sqb[:, rb, :],
                    start=False, stop=True)))(rb),
                    waits=[sq1], sem=TB)
                prev_m1[rb] = m1
                sr = sch.op("s", (lambda rb: (lambda s: s.activation(
                    riprow[:, rb, :], ps_row[:, rb, :], AF.Sqrt)))(rb),
                    waits=[m1], sem=RS)
                prev_sr[rb] = sr
                mx = sch.op("v", (lambda rb: (lambda v: v.tensor_single_scalar(
                    riprow[:, rb, :], riprow[:, rb, :], L2_EPS, OP.max)))(rb),
                    waits=[sr], sem=NM)
                rc = sch.op("v", (lambda rb: (lambda v: v.reciprocal(
                    riprow[:, rb, :], riprow[:, rb, :])))(rb), waits=[mx], sem=NM)
                bc = sch.op("t", (lambda rb: (lambda t: t.matmul(
                    ps_bc[:, rb, :], onesrow[:, :], riprow[:, rb, :],
                    start=True, stop=True)))(rb),
                    waits=[rc, ev_ones2, prev_done.get(rb)], sem=TB)
                done0 = sch.op("v", (lambda sl, t0, rb: (lambda v: v.tensor_mul(
                    qknorm[:, sl, t0, :], convsi[:, sl, t0, :], ps_bc[:, rb, :])))(sl, t0, rb),
                    waits=[bc, ev_qkstore.get(sl)], sem=NM)
                done1 = sch.op("v", (lambda sl, t1, rb: (lambda v: v.tensor_mul(
                    qknorm[:, sl, t1, :], convsi[:, sl, t1, :], ps_bc[:, rb, :])))(sl, t1, rb),
                    waits=[bc, ev_qkstore.get(sl)], sem=NM)
                prev_done[rb] = done1
                dones.append(done0)
                dones.append(done1)
            ev_qkstore[sl] = sch.dma("y",
                dst_dram.rearrange("(t p) c -> p t c", p=128)[:, :, tc * 512:(tc + 1) * 512],
                qknorm[:, sl, :, :], waits=dones, sem=QST[sl])
            return done1

        for part in range(4):
            is_qk = part in (0, 1)
            is_v = part == 2
            is_ctrl = part == 3
            wbuf = part % 2
            if part >= 2:
                ev_w[part] = load_wpart(part, last_mm_of_part[part - 2])
                if part == 3:
                    ev_wab = load_ctrl_ab(last_mm_of_part[2])

            for tc in range(NCHUNK):
                pbuf = tc % 2
                sl = tc % 2
                NFJ = 8
                if not is_ctrl:
                    if tc == 0:
                        ev_halo_cp = None
                        for fj in range(NFJ):
                            bank = mmctr % 3
                            wts = [ev_w[part], ev_xe[0], ev_id, mm_evac.get(mmctr - 3)]
                            ev = None
                            for dj in range(8):
                                ev = sch.op("t", (lambda bank, dj, fj, wbuf: (lambda t: t.matmul(
                                    ps_mm[:, bank, 0:3], Wsb[:, wbuf, dj, fj, :],
                                    xT[:, dj, 0:3],
                                    start=(dj == 0), stop=(dj == 7))))(bank, dj, fj, wbuf),
                                    waits=wts, sem=MM if dj == 7 else None)
                                wts = []
                            last_mm_of_part[part] = ev
                            mm_evac[mmctr] = sch.op(
                                "v", (lambda pbuf, bank, fj: (lambda v: v.tensor_copy(
                                    proj[:, pbuf, fj, 0:3],
                                    ps_mm[:, bank, 0:3])))(pbuf, bank, fj),
                                waits=[ev, last_conv_ev], sem=CV)
                            ev_halo_cp = mm_evac[mmctr]
                            mmctr += 1
                    else:
                        ev_halo_cp = sch.op("v", (lambda pbuf, tc: (lambda v: v.tensor_copy(
                            proj[:, pbuf, :, 0:3],
                            proj[:, 1 - pbuf, :, 512:515])))(pbuf, tc),
                            waits=[mm_evac[mmctr - 1], last_conv_ev], sem=CV)
                og2s = []
                for fj in range(NFJ):
                    bank = mmctr % 3
                    wts = [ev_w[part], ev_xT_all, mm_evac.get(mmctr - 3)]
                    ev = None
                    for dj in range(8):
                        ev = sch.op("t", (lambda bank, dj, fj, tc, wbuf: (lambda t: t.matmul(
                            ps_mm[:, bank, :], Wsb[:, wbuf, dj, fj, :],
                            xT[:, dj, 3 + tc * 512: 3 + (tc + 1) * 512],
                            start=(dj == 0), stop=(dj == 7))))(bank, dj, fj, tc, wbuf),
                            waits=wts, sem=MM if dj == 7 else None)
                        wts = []
                    last_mm_of_part[part] = ev
                    if not is_ctrl:
                        mm_evac[mmctr] = sch.op(
                            "v", (lambda pbuf, bank, fj: (lambda v: v.tensor_copy(
                                proj[:, pbuf, fj, 3:515],
                                ps_mm[:, bank, :])))(pbuf, bank, fj),
                            waits=[ev, last_conv_ev], sem=CV)
                        mmctr += 1
                    else:
                        ob = fj % 2
                        sg = sch.op("s", (lambda bank, fj, ob: (lambda s: s.activation(
                            ogb[:, ob, :], ps_mm[:, bank, :], AF.Sigmoid,
                            bias=bctl[:, fj:fj + 1])))(bank, fj, ob),
                            waits=[ev, ev_bctl, prev_og2], sem=EV)
                        og2 = sch.op("v", (lambda sl, fj, ob: (lambda v: v.tensor_scalar(
                            convsi[:, sl, fj, :], ogb[:, ob, :],
                            ramps[:, fj:fj + 1], None, OP.mult)))(sl, fj, ob),
                            waits=[sg, ev_ramp, ev_consumed.get(sl),
                                   ev_ogstore.get(tc - 2)], sem=NM)
                        prev_og2 = og2
                        mm_evac[mmctr] = og2
                        og2s.append(og2)
                        mmctr += 1

                if is_ctrl:
                    ev_ogstore[tc] = sch.dma("y",
                        ogT.rearrange("(t p) c -> p t c", p=128)[:, :, tc * 512:(tc + 1) * 512],
                        convsi[:, sl, :, :], waits=og2s, sem=OGST[sl])
                    ev_consumed[sl] = ev_ogstore[tc]
                    for k2 in range(2):
                        ab = sl * 2 + k2
                        bank = mmctr % 3
                        wts = [ev_wab, mm_evac.get(mmctr - 3)]
                        ev = None
                        for dj in range(8):
                            ev = sch.op("t", (lambda bank, dj, k2, tc: (lambda t: t.matmul(
                                ps_mm[:, bank, :], Wsb[:, 0, dj, k2, :],
                                xT[:, dj, 3 + tc * 512: 3 + (tc + 1) * 512],
                                start=(dj == 0), stop=(dj == 7))))(bank, dj, k2, tc),
                                waits=wts, sem=MM if dj == 7 else None)
                            wts = []
                        last_mm_of_part[part] = ev
                        sg = sch.op("s", (lambda bank, k2, ab: (lambda s: s.activation(
                            abf[:, ab, :], ps_mm[0:4, bank, :], AF.Sigmoid,
                            bias=bctl[0:4, 8 + k2:9 + k2])))(bank, k2, ab),
                            waits=[ev, ev_abstore.get((tc - 2, k2))], sem=EV)
                        if k2 == 0:
                            sg = sch.op("v", (lambda ab: (lambda v: v.tensor_scalar(
                                abf[:, ab, :], abf[:, ab, :], 0.98, 0.01,
                                OP.mult, OP.add)))(ab), waits=[sg], sem=NM)
                        dst = alpha if k2 == 0 else beta
                        ev_abstore[(tc, k2)] = sch.dma("y",
                            dst[:, tc * 512:(tc + 1) * 512], abf[:, ab, :],
                            waits=[sg], sem=ABST[ab])
                        mm_evac[mmctr] = sg
                        mmctr += 1
                    continue

                ev_ready = {}
                last_cv = None
                for fj in range(NFJ):
                    w0 = [ev_halo_cp, mm_evac[mmctr - NFJ + fj], ev_cw,
                          ev_silu.get((sl, fj))]
                    cv = sch.op("v", (lambda pbuf, sl, fj, part: (lambda v: v.tensor_scalar(
                        convb[:, sl, fj, :], proj[:, pbuf, fj, 0:512],
                        cw[:, part * 8 + fj, 0:1], None, OP.mult)))(pbuf, sl, fj, part),
                        waits=w0, sem=CV)
                    for j in range(1, 4):
                        cv = sch.op("v", (lambda pbuf, sl, fj, j, part: (lambda v: v.scalar_tensor_tensor(
                            convb[:, sl, fj, :], proj[:, pbuf, fj, j:j + 512],
                            cw[:, part * 8 + fj, j:j + 1], convb[:, sl, fj, :],
                            OP.mult, OP.add)))(pbuf, sl, fj, j, part),
                            waits=[cv], sem=CV)
                    last_cv = cv
                    ev_ready[fj] = sch.op("s", (lambda sl, fj: (lambda s: s.activation(
                        convsi[:, sl, fj, :], convb[:, sl, fj, :], AF.Silu)))(sl, fj),
                        waits=[cv, ev_consumed.get(sl)], sem=SI)
                    ev_silu[(sl, fj)] = ev_ready[fj]
                last_conv_ev = last_cv

                if dbg and part == 0 and tc == 1:
                    sch.dma("y", dbg_xT, xT[:, :, 0:640], waits=[ev_xT_all], sem=DBGST)
                    sch.dma("y", dbg_W, Wsb[:, 0, :, :, :], waits=[ev_w[0]], sem=DBGST)
                    sch.dma("y", dbg_proj, proj[:, pbuf, :, :],
                            waits=[mm_evac[mmctr - 1], ev_halo_cp], sem=DBGST)
                    sch.dma("y", dbg_conv, convb[:, sl, :, :],
                            waits=[last_cv], sem=DBGST)
                    sch.dma("y", dbg_si, convsi[:, sl, :, :],
                            waits=[ev_ready[7]], sem=DBGST)
                if is_qk:
                    dstd = qhT if part == 0 else khT
                    ev_consumed[sl] = store_qk(part, tc, dstd, ev_ready)
                else:
                    vtes = []
                    for tt in range(4):
                        evt = None
                        for fj in range(8):
                            evt = sch.op("t", (lambda sl, tt, fj: (lambda t: t.transpose(
                                ps_tr[:, fj * 128:(fj + 1) * 128],
                                convsi[:, sl, fj, tt * 128:(tt + 1) * 128],
                                ident[:, :])))(sl, tt, fj),
                                waits=[ev_ready[fj], prev_vte], sem=XT_)
                        prev_vte = sch.op("v", (lambda sl, tt: (lambda v: v.tensor_copy(
                            vtokb[:, sl, tt, :], ps_tr[:, :])))(sl, tt),
                            waits=[evt, ev_vst.get(sl)], sem=XE)
                        vtes.append(prev_vte)
                    ev_vst[sl] = sch.dma("y",
                        Vout.rearrange("(tc tt p) d -> tc p tt d", tt=4, p=128)[tc],
                        vtokb[:, sl, :, :], waits=vtes, sem=VST[sl])
                    ev_consumed[sl] = prev_vte

        final_waits = [(s, s.n) for s in QST + VST + OGST + ABST + [DBGST]]
        sch.op("y", lambda y: y.nop(), waits=final_waits)

        with nc.Block() as blk:
            sch.finalize(blk)
    return nc



def stage_pre_inputs(inputs):
    x = np.asarray(inputs["L0_attn_x"], dtype=np.float32)
    W_in = f32c(inputs["L0_attn_W_in"])
    W_ctrl = np.asarray(inputs["L0_attn_W_ctrl"], dtype=np.float32)
    b_ctrl = np.asarray(inputs["L0_attn_b_ctrl"], dtype=np.float32)
    Wc = np.zeros((D, 1280), dtype=np.float32)
    Wc[:, 0:1024] = W_ctrl[:, 8:]
    Wc[:, 1024:1028] = W_ctrl[:, 0:4]
    Wc[:, 1152:1156] = W_ctrl[:, 4:8]
    bcT = np.zeros((128, 10), dtype=np.float32)
    for fj in range(8):
        bcT[:, fj] = b_ctrl[8 + fj * 128: 8 + (fj + 1) * 128]
    bcT[0:4, 8] = b_ctrl[0:4]
    bcT[0:4, 9] = b_ctrl[4:8]
    cwall = np.zeros((128, 24, 4), dtype=np.float32)
    for kind, w in enumerate((inputs["L0_attn_w_q"], inputs["L0_attn_w_k"],
                              inputs["L0_attn_w_v"])):
        w = np.asarray(w, dtype=np.float32)[:, 0, :]
        for dj in range(8):
            cwall[:, kind * 8 + dj, :] = w[dj * 128:(dj + 1) * 128, :]
    rampT = np.zeros((128, 8), dtype=np.float32)
    rl = np.asarray(inputs["L0_attn_ramp_logit"], dtype=np.float32)
    for dj in range(8):
        rampT[:, dj] = rl[dj * 128:(dj + 1) * 128]

    in_maps = []
    for core in range(N_CORES):
        b, half = core // 2, core % 2
        t0 = half * SH
        xsl = f32c(x[b, t0:t0 + SH, :])
        if t0 == 0:
            halo = np.zeros((3, D), dtype=np.float32)
        else:
            halo = f32c(x[b, t0 - 3:t0, :])
        in_maps.append({
            "xs": xsl, "xhalo": halo, "W_in": W_in, "W_ctrl": f32c(Wc),
            "b_ctrlT": f32c(bcT), "cwall": f32c(cwall), "rampT": f32c(rampT),
        })
    return in_maps


def kernel(**inputs):
    W0 = stage_l0_weights(inputs)
    W1 = stage_l1_weights(inputs)
    x = np.asarray(inputs["L0_attn_x"], dtype=np.float32)
    cores = list(range(N_CORES))
    nc1 = build_pre()
    im1, xsl = [], {}
    for core in cores:
        b, half = core // 2, core % 2
        t0 = half * SH
        xsl[core] = np.ascontiguousarray(x[b, t0:t0 + SH, :])
        halo = (np.zeros((3, D), np.float32) if t0 == 0
                else np.ascontiguousarray(x[b, t0 - 3:t0, :]))
        im1.append({"xs": xsl[core], "xhalo": halo, "W_in": W0["W_in"],
                    "W_ctrl": W0["W_ctrl"], "b_ctrlT": W0["b_ctrlT"],
                    "cwall": W0["cwall"], "rampT": W0["rampT"]})
    r1 = run_bass_kernel_spmd(nc1, im1, core_ids=cores).results
    nc2 = build_scan()
    im2 = []
    for core in cores:
        b = core // 2
        hsel = slice(0, 512) if core % 2 == 0 else slice(512, 1024)
        h2 = slice(0, 2) if core % 2 == 0 else slice(2, 4)
        qf = np.concatenate([np.asarray(r1[2 * b]["qhT"]),
                             np.asarray(r1[2 * b + 1]["qhT"])], axis=1)
        kf = np.concatenate([np.asarray(r1[2 * b]["khT"]),
                             np.asarray(r1[2 * b + 1]["khT"])], axis=1)
        vf = np.concatenate([np.asarray(r1[2 * b]["V"]),
                             np.asarray(r1[2 * b + 1]["V"])], axis=0)
        af = np.concatenate([np.asarray(r1[2 * b]["alpha"]),
                             np.asarray(r1[2 * b + 1]["alpha"])], axis=1)
        bf_ = np.concatenate([np.asarray(r1[2 * b]["beta"]),
                              np.asarray(r1[2 * b + 1]["beta"])], axis=1)
        im2.append({"qhT": np.ascontiguousarray(qf[hsel]),
                    "khT": np.ascontiguousarray(kf[hsel]),
                    "V": np.ascontiguousarray(vf[:, hsel]),
                    "alpha": np.ascontiguousarray(af[h2]),
                    "beta": np.ascontiguousarray(bf_[h2])})
    r2 = run_bass_kernel_spmd(nc2, im2, core_ids=cores).results
    nc3 = build_l0proj()
    im3 = []
    for core in cores:
        b, half = core // 2, core % 2
        t0 = half * SH
        rh_full = np.concatenate([np.asarray(r2[2 * b]["RhT"]),
                                  np.asarray(r2[2 * b + 1]["RhT"])], axis=0)
        im3.append({"RhT": np.ascontiguousarray(rh_full[:, t0:t0 + SH]),
                    "ogT": np.asarray(r1[core]["ogT"]),
                    "xs": xsl[core], "W_out": W0["W_out"],
                    "b_outT": W0["b_outT"], "centT": W0["centT"]})
    r3 = run_bass_kernel_spmd(nc3, im3, core_ids=cores).results
    nc4 = build_l0ffn()
    im4 = [{"x1": np.asarray(r3[c]["x1"]), "xhatT": np.asarray(r3[c]["xhatT"]),
            "mask": np.asarray(r3[c]["mask"]),
            "ln_gT": W0["ln_gT"], "ln_bT": W0["ln_bT"], "w1T": W0["w1T"],
            "b1T": W0["b1T"], "w2T": W0["w2T"], "b2r": W0["b2r"]}
           for c in cores]
    r4 = run_bass_kernel_spmd(nc4, im4, core_ids=cores).results
    x2 = [np.asarray(r4[c]["x2"]) for c in cores]
    nc5 = build_l1attn()
    im5 = []
    for core in cores:
        b = core // 2
        im5.append({"x2kv": x2[core], "x2q1": x2[2 * b], "x2q2": x2[2 * b + 1],
                    "Wq": W1["Wq"], "Wk": W1["Wk"], "Wv": W1["Wv"],
                    "bqT": W1["bqT"], "bkT": W1["bkT"], "bvr": W1["bvr"]})
    r5 = run_bass_kernel_spmd(nc5, im5, core_ids=cores).results
    nc6 = build_l1tail()
    im6 = []
    for core in cores:
        b, half = core // 2, core % 2
        t0 = half * SH
        pA = np.asarray(r5[2 * b]["pctxT"])
        pB = np.asarray(r5[2 * b + 1]["pctxT"])
        im6.append({"pctxA": np.ascontiguousarray(pA[:, t0:t0 + SH]),
                    "pctxB": np.ascontiguousarray(pB[:, t0:t0 + SH]),
                    "x2": x2[core],
                    "wgT": W1["wgT"], "w1eT": W1["w1eT"], "w3eT": W1["w3eT"],
                    "w2eT": W1["w2eT"], "w1sT": W1["w1sT"],
                    "w3sT": W1["w3sT"], "w2sT": W1["w2sT"]})
    r6 = run_bass_kernel_spmd(nc6, im6, core_ids=cores).results
    out = np.zeros((B, S, D), np.float32)
    for core in cores:
        b, half = core // 2, core % 2
        t0 = half * SH
        out[b, t0:t0 + SH] = np.asarray(r6[core]["out"])
    return out



def build_scan(nchunk=None):
    SQ = S
    if nchunk is None:
        nchunk = SQ // C
    nc = bass.Bass("TRN2", target_bir_lowering=False, debug=False)
    qhT = nc.dram_tensor("qhT", [2 * HD, SQ], BF16, kind="ExternalInput").ap()
    khT = nc.dram_tensor("khT", [2 * HD, SQ], BF16, kind="ExternalInput").ap()
    Vd = nc.dram_tensor("V", [SQ, 2 * HD], BF16, kind="ExternalInput").ap()
    alpha = nc.dram_tensor("alpha", [2, SQ], F32, kind="ExternalInput").ap()
    beta = nc.dram_tensor("beta", [2, SQ], F32, kind="ExternalInput").ap()
    RhT = nc.dram_tensor("RhT", [2 * HD, SQ], BF16, kind="ExternalOutput").ap()

    NCH = nchunk
    with contextlib.ExitStack() as stack:
        sb = lambda *a: stack.enter_context(nc.sbuf_tensor(*a))
        psb = lambda *a: stack.enter_context(nc.psum_tensor(*a))
        identb = sb("identb", [128, 128], BF16)
        identf = sb("identf", [128, 128], F32)
        ones1r = sb("ones1r", [1, 128], F32)
        cumtri = sb("cumtri", [128, 128], F32)
        botrow = sb("botrow", [128, 128], F32)
        mask_sl = sb("mask_sl", [128, 128], F32)
        mask_su = sb("mask_su", [128, 128], F32)
        mask_ui = sb("mask_ui", [128, 128], F32)
        ab_sb = sb("ab_sb", [4, SQ], F32)
        abcols = sb("abcols", [128, NCH, 4], F32)
        lgac = sb("lgac", [128, NCH, 2], F32)
        qbuf = sb("qbuf", [128, 2, 4, 128], BF16)
        kbuf = sb("kbuf", [128, 2, 4, 128], BF16)
        vbuf = sb("vbuf", [128, 2, 512], BF16)
        lg_sb = sb("lg_sb", [128, 2, 8], F32)
        rowt = sb("rowt", [1, 2, 4, 128], F32)
        bcast = sb("bcast", [128, 2, 3, 128], F32)
        dmat = sb("dmat", [128, 2, 4, 128], F32)
        Tb = sb("Tb", [128, 2, 2, 128], BF16)
        Pw = sb("Pw", [128, 2, 2, 128], BF16)
        Mb = sb("Mb", [128, 2, 128], BF16)
        Ktok = sb("Ktok", [128, 2, 256], BF16)
        QhT = sb("QhT", [128, 2, 2, 128], BF16)
        Wsb2 = sb("Wsb2", [128, 2, 256], BF16)
        Usb = sb("Usb", [128, 2, 256], BF16)
        Utl = sb("Utl", [128, 2, 256], BF16)
        STb = sb("STb", [128, 2, 2, 2, 256], BF16)
        rhat = sb("rhat", [128, 2, 256], BF16)
        gIb = sb("gIb", [128, 2, 128], BF16)
        rout = sb("rout", [128, 2, 4, 128], BF16)
        sqr = sb("sqr", [128, 2, 256], F32)
        msb = sb("msb", [128, 2, 4], F32)
        epsb = sb("epsb", [128, 1], F32)

        ps_f = [psb("ps_f0", [128, 3, 512], F32), psb("ps_f1", [128, 3, 512], F32)]
        ps_b = [psb("ps_b0", [128, 1024], BF16), psb("ps_b1", [128, 1024], BF16)]

        sch = Sched(nc, stack)
        CONST = sch.sem("dma")
        QLD = [sch.sem("dma"), sch.sem("dma")]
        KLD = [sch.sem("dma"), sch.sem("dma")]
        VLD = [sch.sem("dma"), sch.sem("dma")]
        RST = [sch.sem("dma"), sch.sem("dma")]
        GI = sch.sem("g")
        TE = sch.sem("t")
        VE = sch.sem("v")
        SE = sch.sem("s")

        ev_idb = emit_identity_ops(sch, "g", identb[:, :], GI)
        ev_idf = emit_identity_ops(sch, "g", identf[:, :], GI)
        ev_o1 = sch.op("g", lambda g: g.memset(ones1r[:, :], 1.0), sem=GI)
        ev_ct = emit_tri_ops(sch, "g", cumtri[:, :], "upper_incl", GI)
        ev_br0 = sch.op("g", lambda g: g.memset(botrow[:, :], 0.0), sem=GI)
        ev_br = sch.op("g", lambda g: g.affine_select(
            out=botrow[:, :], in_=botrow[:, :], compare_op=OP.not_equal,
            fill=1.0, base=-127, pattern=[[0, 128]], channel_multiplier=1),
            waits=[ev_br0], sem=GI)
        ev_sl = emit_tri_ops(sch, "g", mask_sl[:, :], "lower_strict", GI)
        ev_su = emit_tri_ops(sch, "g", mask_su[:, :], "upper_strict", GI)
        ev_ui = emit_tri_ops(sch, "g", mask_ui[:, :], "upper_incl", GI)
        ev_ab0 = sch.dma("g", ab_sb[0:2, :], alpha, sem=CONST)
        ev_ab = sch.dma("g", ab_sb[2:4, :], beta, sem=CONST)
        ev_consts = ev_ab

        ev_abt = None
        prev_evac = None
        for c in range(NCH):
            evt = sch.op("t", (lambda c: (lambda t: t.transpose(
                ps_f[0][:, 0, 0:4], ab_sb[:, c * 128:(c + 1) * 128],
                identf[0:4, 0:4])))(c),
                waits=[ev_consts, ev_idf, prev_evac], sem=TE)
            prev_evac = sch.op("v", (lambda c: (lambda v: v.tensor_copy(
                abcols[:, c, :], ps_f[0][:, 0, 0:4])))(c),
                waits=[evt], sem=VE)
        ev_abcols = prev_evac
        ev_lga = sch.op("s", lambda s: s.activation(
            lgac[:, :, :], abcols[:, :, 0:2], AF.Ln),
            waits=[ev_abcols], sem=SE)

        ev_st0 = sch.op("g", lambda g: g.memset(STb[:, :, :, :, :], 0.0), sem=GI)
        ev_eps = sch.op("g", lambda g: g.memset(epsb[:, :], RMS_EPS), sem=GI)

        ev_qld = {}
        ev_kld = {}
        ev_vld = {}
        ev_qfree = {}
        ev_kfree = {}
        ev_vfree = {}

        def load_chunk(c):
            sl = c % 2
            ev_qld[c] = sch.dma("g", qbuf[:, sl, :, :],
                                qhT.rearrange("(j p) c -> p j c", p=128)[:, :, c * 128:(c + 1) * 128],
                                waits=ev_qfree.get(c - 2, []), sem=QLD[sl])
            ev_kld[c] = sch.dma("g", kbuf[:, sl, :, :],
                                khT.rearrange("(j p) c -> p j c", p=128)[:, :, c * 128:(c + 1) * 128],
                                waits=ev_kfree.get(c - 2, []), sem=KLD[sl])
            ev_vld[c] = sch.dma("g", vbuf[:, sl, :],
                                Vd[c * 128:(c + 1) * 128, :],
                                waits=ev_vfree.get(c - 2, []), sem=VLD[sl])

        load_chunk(0)
        if NCH > 1:
            load_chunk(1)

        ev_st = {(h): ev_st0 for h in range(2)}
        ev_st_read = {h: None for h in range(2)}
        ev_rst = {}
        ev_rout_free = {}
        prev = {}
        psum_guard = {}
        psum_guard[(0, 0)] = ev_abcols
        psb_guard = {}

        for c in range(NCH):
            sl = c % 2
            rsl = c % 2
            ev_rall = []
            for h in range(2):
                pf = ps_f[h]
                pb = ps_b[h]
                QT = qbuf[:, sl, 2 * h:2 * h + 2, :]
                KT = kbuf[:, sl, 2 * h:2 * h + 2, :]
                Vt = vbuf[:, sl, h * 256:(h + 1) * 256]
                acol = abcols[:, c, h:h + 1]
                bcol = abcols[:, c, 2 + h:3 + h]
                lgacol = lgac[:, c, h:h + 1]
                stbuf_old = c % 2
                stbuf_new = 1 - c % 2
                STold = STb[:, h, stbuf_old, :, :]
                STnew = STb[:, h, stbuf_new, :, :]

                evKK = None
                for j in range(2):
                    evKK = sch.op("t", (lambda h, j, KT: (lambda t: t.matmul(
                        ps_f[h][:, 0, 0:128], KT[:, j, :], KT[:, j, :],
                        start=(j == 0), stop=(j == 1))))(h, j, KT),
                        waits=[ev_kld[c], psum_guard.get((h, 0))],
                        sem=TE if j == 1 else None)
                evQK = None
                for j in range(2):
                    evQK = sch.op("t", (lambda h, j, KT, QT: (lambda t: t.matmul(
                        ps_f[h][:, 0, 128:256], KT[:, j, :], QT[:, j, :],
                        start=(j == 0), stop=(j == 1))))(h, j, KT, QT),
                        waits=[ev_qld[c]], sem=TE if j == 1 else None)
                evlg = sch.op("t", (lambda h, lgacol: (lambda t: t.matmul(
                    ps_f[h][:, 1, 0:1], cumtri[:, :], lgacol,
                    start=True, stop=True)))(h, lgacol),
                    waits=[ev_lga, ev_ct, psum_guard.get((h, 1))], sem=TE)
                evlg_sb = sch.op("v", (lambda h: (lambda v: v.tensor_copy(
                    lg_sb[:, h, 0:1], ps_f[h][:, 1, 0:1])))(h),
                    waits=[evlg], sem=VE)
                evbot = sch.op("t", (lambda h: (lambda t: t.matmul(
                    ps_f[h][:, 1, 1:2], botrow[:, :], lg_sb[:, h, 0:1],
                    start=True, stop=True)))(h),
                    waits=[evlg_sb, ev_br], sem=TE)
                evbot_sb = sch.op("v", (lambda h: (lambda v: v.tensor_copy(
                    lg_sb[:, h, 1:2], ps_f[h][:, 1, 1:2])))(h),
                    waits=[evbot], sem=VE)
                psum_guard[(h, 1)] = evbot_sb
                evg = sch.op("s", (lambda h: (lambda s: s.activation(
                    lg_sb[:, h, 2:3], lg_sb[:, h, 0:1], AF.Exp)))(h),
                    waits=[evlg_sb], sem=SE)
                evbg = sch.op("v", (lambda h, bcol: (lambda v: v.tensor_mul(
                    lg_sb[:, h, 3:4], bcol, lg_sb[:, h, 2:3])))(h, bcol),
                    waits=[evg, ev_abcols], sem=VE)
                evd = sch.op("v", (lambda h: (lambda v: v.tensor_sub(
                    lg_sb[:, h, 4:5], lg_sb[:, h, 1:2], lg_sb[:, h, 0:1])))(h),
                    waits=[evbot_sb], sem=VE)
                evug = sch.op("s", (lambda h: (lambda s: s.activation(
                    lg_sb[:, h, 4:5], lg_sb[:, h, 4:5], AF.Exp)))(h),
                    waits=[evd], sem=SE)
                evgam = sch.op("s", (lambda h: (lambda s: s.activation(
                    lg_sb[:, h, 5:6], lg_sb[:, h, 1:2], AF.Exp)))(h),
                    waits=[evbot_sb, evug], sem=SE)
                evt1 = sch.op("t", (lambda h: (lambda t: t.transpose(
                    ps_f[h][0:1, 1, 128:256], lg_sb[:, h, 0:1], identf[:, :])))(h),
                    waits=[evlg_sb, evbot_sb], sem=TE)
                evt1e = sch.op("v", (lambda h: (lambda v: v.tensor_copy(
                    rowt[:, h, 0, :], ps_f[h][0:1, 1, 128:256])))(h),
                    waits=[evt1], sem=VE)
                evt2 = sch.op("t", (lambda h, bcol: (lambda t: t.transpose(
                    ps_f[h][0:1, 1, 256:384], bcol, identf[:, :])))(h, bcol),
                    waits=[evt1e], sem=TE)
                evt2e = sch.op("v", (lambda h: (lambda v: v.tensor_copy(
                    rowt[:, h, 1, :], ps_f[h][0:1, 1, 256:384])))(h),
                    waits=[evt2], sem=VE)
                evb1 = sch.op("t", (lambda h: (lambda t: t.matmul(
                    ps_f[h][:, 1, 128:256], ones1r[:, :], rowt[:, h, 0, :],
                    start=True, stop=True)))(h),
                    waits=[evt1e, ev_o1, evt2], sem=TE)
                evb1e = sch.op("v", (lambda h: (lambda v: v.tensor_copy(
                    bcast[:, h, 0, :], ps_f[h][:, 1, 128:256])))(h),
                    waits=[evb1], sem=VE)
                evb2 = sch.op("t", (lambda h: (lambda t: t.matmul(
                    ps_f[h][:, 1, 256:384], ones1r[:, :], rowt[:, h, 1, :],
                    start=True, stop=True)))(h),
                    waits=[evt2e, evb1], sem=TE)
                evb2e = sch.op("v", (lambda h: (lambda v: v.tensor_copy(
                    bcast[:, h, 1, :], ps_f[h][:, 1, 256:384])))(h),
                    waits=[evb2], sem=VE)
                psum_guard[(h, 1)] = evb2e
                ev1 = sch.op("v", (lambda h: (lambda v: v.tensor_scalar(
                    dmat[:, h, 0, :], bcast[:, h, 0, :], lg_sb[:, h, 0:1],
                    None, OP.subtract)))(h), waits=[evb1e], sem=VE)
                ev2 = sch.op("v", (lambda h: (lambda v: v.tensor_single_scalar(
                    dmat[:, h, 0, :], dmat[:, h, 0, :], 0.0, OP.min)))(h),
                    waits=[ev1], sem=VE)
                ev3 = sch.op("s", (lambda h: (lambda s: s.activation(
                    dmat[:, h, 0, :], dmat[:, h, 0, :], AF.Exp)))(h),
                    waits=[ev2], sem=SE)
                ev4 = sch.op("v", (lambda h: (lambda v: v.tensor_mul(
                    dmat[:, h, 0, :], dmat[:, h, 0, :], mask_ui[:, :])))(h),
                    waits=[ev3, ev_ui], sem=VE)
                ev5 = sch.op("v", (lambda h: (lambda v: v.tensor_scalar(
                    dmat[:, h, 1, :], bcast[:, h, 0, :], -1.0, lg_sb[:, h, 0:1],
                    OP.mult, OP.add)))(h), waits=[evb1e], sem=VE)
                ev6 = sch.op("v", (lambda h: (lambda v: v.tensor_single_scalar(
                    dmat[:, h, 1, :], dmat[:, h, 1, :], 0.0, OP.min)))(h),
                    waits=[ev5], sem=VE)
                ev7 = sch.op("s", (lambda h: (lambda s: s.activation(
                    dmat[:, h, 1, :], dmat[:, h, 1, :], AF.Exp)))(h),
                    waits=[ev6], sem=SE)
                ev8 = sch.op("v", (lambda h: (lambda v: v.tensor_mul(
                    dmat[:, h, 1, :], dmat[:, h, 1, :], mask_sl[:, :])))(h),
                    waits=[ev7, ev_sl], sem=VE)
                ev9 = sch.op("v", (lambda h: (lambda v: v.tensor_mul(
                    dmat[:, h, 2, :], dmat[:, h, 0, :], mask_su[:, :])))(h),
                    waits=[ev4, ev_su], sem=VE)
                ev10 = sch.op("v", (lambda h: (lambda v: v.tensor_mul(
                    dmat[:, h, 2, :], dmat[:, h, 2, :], bcast[:, h, 1, :])))(h),
                    waits=[ev9, evb2e], sem=VE)
                evT = sch.op("v", (lambda h, bcol: (lambda v: v.scalar_tensor_tensor(
                    Tb[:, h, 0, :], ps_f[h][:, 0, 0:128], bcol,
                    dmat[:, h, 1, :], OP.mult, OP.mult)))(h, bcol),
                    waits=[evKK, evQK, ev8, prev.get(("T", h))], sem=VE)
                evTp = sch.op("v", (lambda h: (lambda v: v.tensor_mul(
                    Tb[:, h, 1, :], ps_f[h][:, 0, 0:128], dmat[:, h, 2, :])))(h),
                    waits=[evKK, ev10, prev.get(("Tp", h)),
                           prev.get(("Tp2", h))], sem=VE)
                evM = sch.op("v", (lambda h: (lambda v: v.tensor_mul(
                    Mb[:, h, :], ps_f[h][:, 0, 128:256], dmat[:, h, 0, :])))(h),
                    waits=[evQK, ev4, prev.get(("M", h))], sem=VE)
                psum_guard[(h, 0)] = evM
                evgrow = sch.op("s", (lambda h: (lambda s: s.activation(
                    bcast[:, h, 2, :], bcast[:, h, 0, :], AF.Exp)))(h),
                    waits=[evb1e, prev.get(("grow", h))], sem=SE)
                evq1 = sch.op("v", (lambda h, QT: (lambda v: v.tensor_mul(
                    QhT[:, h, 0, :], QT[:, 0, :], bcast[:, h, 2, :])))(h, QT),
                    waits=[evgrow, ev_qld[c], prev.get(("Qh", h))], sem=VE)
                evq2 = sch.op("v", (lambda h, QT: (lambda v: v.tensor_mul(
                    QhT[:, h, 1, :], QT[:, 1, :], bcast[:, h, 2, :])))(h, QT),
                    waits=[evgrow], sem=VE)
                evkt = None
                for j in range(2):
                    evkt = sch.op("t", (lambda h, j, KT: (lambda t: t.transpose(
                        ps_b[h][:, j * 128:(j + 1) * 128], KT[:, j, :],
                        identb[:, :])))(h, j, KT),
                        waits=[ev_kld[c], ev_idb, psb_guard.get((h, 0)),
                               psb_guard.get((h, 1))],
                        sem=TE if j == 1 else None)
                evkte = sch.op("v", (lambda h: (lambda v: v.tensor_copy(
                    Ktok[:, h, :], ps_b[h][:, 0:256])))(h),
                    waits=[evkt, prev.get(("Ktok", h))], sem=VE)
                psb_guard[(h, 0)] = evkte
                ev_kfree.setdefault(c, []).append(evkte)
                evcp0 = sch.op("v", (lambda h: (lambda v: v.tensor_copy(
                    Pw[:, h, 0, :], Tb[:, h, 0, :])))(h),
                    waits=[evT, prev.get(("Pw", h))], sem=VE)
                evcp1 = sch.op("v", (lambda h: (lambda v: v.tensor_copy(
                    Pw[:, h, 1, :], Tb[:, h, 1, :])))(h),
                    waits=[evTp, prev.get(("Pw", h))], sem=VE)
                evks = None
                for j in range(2):
                    evks = sch.op("t", (lambda h, j, KT, c: (lambda t: t.matmul(
                        ps_f[h][:, 2, 0:256], KT[:, j, :],
                        STb[:, h, c % 2, j, :],
                        start=(j == 0), stop=(j == 1))))(h, j, KT, c),
                        waits=[ev_st[h], psum_guard.get((h, 2)), ev_kld[c]],
                        sem=TE if j == 1 else None)
                evw1 = sch.op("v", (lambda h, Vt, bcol: (lambda v: v.tensor_scalar(
                    sqr[:, h, :], Vt, bcol, None, OP.mult)))(h, Vt, bcol),
                    waits=[ev_vld[c], prev.get(("sqr", h))], sem=VE)
                evbgn = sch.op("v", (lambda h: (lambda v: v.tensor_scalar(
                    lg_sb[:, h, 6:7], lg_sb[:, h, 3:4], -1.0, None, OP.mult)))(h),
                    waits=[evbg], sem=VE)
                evW = sch.op("v", (lambda h: (lambda v: v.scalar_tensor_tensor(
                    Wsb2[:, h, :], ps_f[h][:, 2, 0:256], lg_sb[:, h, 6:7],
                    sqr[:, h, :], OP.mult, OP.add)))(h),
                    waits=[evks, evw1, evbgn, prev.get(("W", h))], sem=VE)
                ev_vfree.setdefault(c, []).append(evW)
                ev_kfree.setdefault(c, []).append(evks)
                psum_guard[(h, 2)] = evW
                evA = sch.op("t", (lambda h: (lambda t: t.matmul(
                    ps_f[h][:, 2, 256:512], Tb[:, h, 1, :], Wsb2[:, h, :],
                    start=True, stop=True)))(h),
                    waits=[evTp, evW], sem=TE)
                evU = sch.op("v", (lambda h: (lambda v: v.tensor_sub(
                    Usb[:, h, :], Wsb2[:, h, :], ps_f[h][:, 2, 256:512])))(h),
                    waits=[evA, prev.get(("U", h))], sem=VE)
                evA_first = evA
                for j in range(1, 7):
                    evs1 = sch.op("t", (lambda h: (lambda t: t.matmul(
                        ps_f[h][:, 0, 0:128], Tb[:, h, 0, :] if False else
                        Pw[:, h, 0, :], Pw[:, h, 1, :],
                        start=True, stop=True)))(h),
                        waits=[evcp0, evcp1, psum_guard.get((h, 0))], sem=TE)
                    evs2 = sch.op("t", (lambda h: (lambda t: t.matmul(
                        ps_f[h][:, 0, 128:256], Pw[:, h, 1, :], Pw[:, h, 0, :],
                        start=True, stop=True)))(h),
                        waits=[], sem=TE)
                    evcp1 = sch.op("v", (lambda h: (lambda v: v.tensor_copy(
                        Pw[:, h, 1, :], ps_f[h][:, 0, 0:128])))(h),
                        waits=[evs1, evs2], sem=VE)
                    evcp0 = sch.op("v", (lambda h: (lambda v: v.tensor_copy(
                        Pw[:, h, 0, :], ps_f[h][:, 0, 128:256])))(h),
                        waits=[evs2, evcp1], sem=VE)
                    psum_guard[(h, 0)] = evcp0
                    evA = sch.op("t", (lambda h: (lambda t: t.matmul(
                        ps_f[h][:, 2, 256:512], Pw[:, h, 1, :], Usb[:, h, :],
                        start=True, stop=True)))(h),
                        waits=[evcp1, evU, psum_guard.get((h, 2))], sem=TE)
                    evU = sch.op("v", (lambda h: (lambda v: v.tensor_add(
                        Usb[:, h, :], Usb[:, h, :], ps_f[h][:, 2, 256:512])))(h),
                        waits=[evA], sem=VE)
                    psum_guard[(h, 2)] = evU
                evUt = sch.op("v", (lambda h: (lambda v: v.tensor_scalar(
                    Utl[:, h, :], Usb[:, h, :], lg_sb[:, h, 4:5],
                    None, OP.mult)))(h),
                    waits=[evU, evug, prev.get(("Utl", h))], sem=VE)
                evR = sch.op("t", (lambda h: (lambda t: t.matmul(
                    ps_f[h][:, 1, 0:256], Mb[:, h, :], Usb[:, h, :],
                    start=True, stop=False)))(h),
                    waits=[evM, evU, psum_guard.get((h, 1))], sem=TE)
                evR_first = evR
                for j in range(2):
                    evR = sch.op("t", (lambda h, j, c: (lambda t: t.matmul(
                        ps_f[h][:, 1, 0:256], QhT[:, h, j, :],
                        STb[:, h, c % 2, j, :],
                        start=False, stop=(j == 1))))(h, j, c),
                        waits=[evq1, evq2, ev_st[h]], sem=TE if j == 1 else None)
                evsq = sch.op("s", (lambda h: (lambda s: s.activation(
                    sqr[:, h, :], ps_f[h][:, 1, 0:256], AF.Square)))(h),
                    waits=[evR, evW], sem=SE)
                evms = sch.op("v", (lambda h: (lambda v: v.reduce_sum(
                    msb[:, h, 0:1], sqr[:, h, :], axis=AX.X)))(h),
                    waits=[evsq], sem=VE)
                evln = sch.op("s", (lambda h: (lambda s: s.activation(
                    msb[:, h, 1:2], msb[:, h, 0:1], AF.Ln,
                    bias=epsb[:, 0:1], scale=1.0 / 256.0)))(h),
                    waits=[evms, ev_eps], sem=SE)
                evrs = sch.op("s", (lambda h: (lambda s: s.activation(
                    msb[:, h, 2:3], msb[:, h, 1:2], AF.Exp, scale=-0.5)))(h),
                    waits=[evln], sem=SE)
                evrh = sch.op("v", (lambda h: (lambda v: v.tensor_scalar(
                    rhat[:, h, :], ps_f[h][:, 1, 0:256], msb[:, h, 2:3],
                    None, OP.mult)))(h),
                    waits=[evrs, prev.get(("rhat", h))], sem=VE)
                psum_guard[(h, 1)] = evrh
                evrt = None
                for j in range(2):
                    evrt = sch.op("t", (lambda h, j: (lambda t: t.transpose(
                        ps_b[h][:, 512 + j * 128: 512 + (j + 1) * 128],
                        rhat[:, h, j * 128:(j + 1) * 128], identb[:, :])))(h, j),
                        waits=[evrh, psb_guard.get((h, 1)), evkte],
                        sem=TE if j == 1 else None)
                evrte = sch.op("v", (lambda h, rsl: (lambda v: v.tensor_copy(
                    rout[:, rsl, 2 * h:2 * h + 2, :], ps_b[h][:, 512:768])))(h, rsl),
                    waits=[evrt, ev_rout_free.get((rsl, h))], sem=VE)
                psb_guard[(h, 1)] = evrte
                ev_rall.append(evrte)
                evgi = sch.op("v", (lambda h: (lambda v: v.tensor_scalar(
                    gIb[:, h, :], identb[:, :],
                    lg_sb[:, h, 5:6], None, OP.mult)))(h),
                    waits=[evgam, ev_idb, ev_st_read.get(h)], sem=VE)
                ev_stn = None
                for j in range(2):
                    e1 = sch.op("t", (lambda h, j: (lambda t: t.matmul(
                        ps_f[h][:, 2, 0:256] if j == 0 else ps_f[h][:, 2, 256:512],
                        Ktok[:, h, j * 128:(j + 1) * 128], Utl[:, h, :],
                        start=True, stop=False)))(h, j),
                        waits=[evkte, evUt, psum_guard.get((h, 2))], sem=None)
                    ev_stn = sch.op("t", (lambda h, j, c: (lambda t: t.matmul(
                        ps_f[h][:, 2, 0:256] if j == 0 else ps_f[h][:, 2, 256:512],
                        gIb[:, h, :],
                        STb[:, h, c % 2, j, :],
                        start=False, stop=True)))(h, j, c),
                        waits=[evgi, ev_st[h]], sem=TE)
                evst1 = sch.op("v", (lambda h, c: (lambda v: v.tensor_copy(
                    STb[:, h, 1 - c % 2, 0, :], ps_f[h][:, 2, 0:256])))(h, c),
                    waits=[ev_stn, ev_st_read.get(h)], sem=VE)
                evst2 = sch.op("v", (lambda h, c: (lambda v: v.tensor_copy(
                    STb[:, h, 1 - c % 2, 1, :], ps_f[h][:, 2, 256:512])))(h, c),
                    waits=[evst1], sem=VE)
                psum_guard[(h, 2)] = evst2
                ev_st[h] = evst2
                ev_st_read[h] = ev_stn
                prev[("T", h)] = evcp0
                prev[("Tp", h)] = evcp1
                prev[("Tp2", h)] = evA_first
                prev[("M", h)] = evR_first
                prev[("Qh", h)] = evR
                prev[("Ktok", h)] = ev_stn
                prev[("Pw", h)] = evA
                prev[("sqr", h)] = evms
                prev[("W", h)] = evU
                prev[("U", h)] = evUt
                prev[("Utl", h)] = ev_stn
                prev[("rhat", h)] = evrt
                prev[("grow", h)] = evq2
                ev_qfree.setdefault(c, []).append(evq2)
                ev_qfree.setdefault(c, []).append(evQK)

            ev_rst[c] = sch.dma("y",
                RhT.rearrange("(j p) c -> p j c", p=128)[:, :, c * 128:(c + 1) * 128],
                rout[:, rsl, :, :], waits=ev_rall, sem=RST[rsl])
            for h in range(2):
                ev_rout_free[(rsl, h)] = ev_rst[c]
            if c + 2 < NCH:
                load_chunk(c + 2)

        sch.op("y", lambda y: y.nop(),
               waits=[(s, s.n) for s in RST])
        with nc.Block() as blk:
            sch.finalize(blk)
    return nc



def emit_ln(sch, x_ap, out_ap, tmp_ap, red_ap, epsb_ap, waits, VE, SE,
            scale=1.0 / 1024.0):
    e1 = sch.op("v", (lambda: (lambda v: v.reduce_sum(
        red_ap[:, 0:1], x_ap, axis=AX.X, negate=True)))(), waits=waits, sem=VE)
    e2 = sch.op("v", (lambda: (lambda v: v.tensor_scalar(
        red_ap[:, 1:2], red_ap[:, 0:1], scale, None, OP.mult)))(),
        waits=[e1], sem=VE)
    e3 = sch.op("v", (lambda: (lambda v: v.tensor_scalar(
        tmp_ap, x_ap, red_ap[:, 1:2], None, OP.add)))(), waits=[e2], sem=VE)
    e4 = sch.op("s", (lambda: (lambda s: s.activation(
        out_ap, tmp_ap, AF.Square)))(), waits=[e3], sem=SE)
    e5 = sch.op("v", (lambda: (lambda v: v.reduce_sum(
        red_ap[:, 0:1], out_ap, axis=AX.X)))(), waits=[e4], sem=VE)
    e6 = sch.op("s", (lambda: (lambda s: s.activation(
        red_ap[:, 1:2], red_ap[:, 0:1], AF.Ln, bias=epsb_ap,
        scale=scale)))(), waits=[e5], sem=SE)
    e7 = sch.op("s", (lambda: (lambda s: s.activation(
        red_ap[:, 2:3], red_ap[:, 1:2], AF.Exp, scale=-0.5)))(),
        waits=[e6], sem=SE)
    e8 = sch.op("v", (lambda: (lambda v: v.tensor_scalar(
        out_ap, tmp_ap, red_ap[:, 2:3], None, OP.mult)))(),
        waits=[e7], sem=VE)
    return e8



def build_l0proj():
    nc = bass.Bass("TRN2", target_bir_lowering=False, debug=False)
    RhT = nc.dram_tensor("RhT", [D, SH], BF16, kind="ExternalInput").ap()
    ogT = nc.dram_tensor("ogT", [D, SH], BF16, kind="ExternalInput").ap()
    xs = nc.dram_tensor("xs", [SH, D], F32, kind="ExternalInput").ap()
    W_out = nc.dram_tensor("W_out", [D, D], F32, kind="ExternalInput").ap()
    b_outT = nc.dram_tensor("b_outT", [128, 8], F32, kind="ExternalInput").ap()
    centT = nc.dram_tensor("centT", [D, E0], F32, kind="ExternalInput").ap()
    x1o = nc.dram_tensor("x1", [SH, D], F32, kind="ExternalOutput").ap()
    xhato = nc.dram_tensor("xhatT", [D, SH], BF16, kind="ExternalOutput").ap()
    masko = nc.dram_tensor("mask", [SH, E0 + 1], F32, kind="ExternalOutput").ap()

    W_out_r = W_out.rearrange("(dj p) (fj q) -> dj p fj q", p=128, q=128)

    with contextlib.ExitStack() as stack:
        sb = lambda *a: stack.enter_context(nc.sbuf_tensor(*a))
        psb = lambda *a: stack.enter_context(nc.psum_tensor(*a))
        identb = sb("identb", [128, 128], BF16)
        identf = sb("identf", [128, 128], F32)
        Wo = sb("Wo", [128, 8, 8, 128], BF16)
        rhbuf = sb("rhbuf", [128, 2, 8, 512], BF16)
        ogbuf = sb("ogbuf", [128, 2, 8, 512], BF16)
        attnT = sb("attnT", [128, 8, 512], BF16)
        tstage = sb("tstage", [128, 512], F32)
        xsb = sb("xsb", [128, 2, 4, D], F32)
        x1b = sb("x1b", [128, 4, D], F32)
        xhb = sb("xhb", [128, D], BF16)
        xtb = sb("xtb", [128, 4, 8, 128], BF16)
        x1tf = sb("x1tf", [128, 8, 128], F32)
        tmpf = sb("tmpf", [128, D], F32)
        lnsq = sb("lnsq", [128, D], F32)
        redb = sb("redb", [128, 8], F32)
        epsb = sb("epsb", [128, 1], F32)
        centb = sb("centb", [128, 8, 8], F32)
        afb = sb("afb", [128, NT, 10], F32)
        maskb = sb("maskb", [128, NT, 9], F32)
        bob = sb("bob", [128, 8], F32)

        ps_mm = psb("ps_mm", [128, 3, 512], F32)
        ps_tr = psb("ps_tr", [128, 2, 1024], BF16)
        ps_aff = psb("ps_aff", [128, 512], F32)

        sch = Sched(nc, stack)
        CONST = sch.sem("dma")
        RLD = [sch.sem("dma"), sch.sem("dma")]
        OLD = [sch.sem("dma"), sch.sem("dma")]
        XLD = [sch.sem("dma"), sch.sem("dma")]
        X1ST = sch.sem("dma")
        XHST = sch.sem("dma")
        RTST = sch.sem("dma")
        GI = sch.sem("g")
        TE = sch.sem("t")
        VE = sch.sem("v")
        SE = sch.sem("s")

        ev_id = emit_identity_ops(sch, "g", identb[:, :], GI)
        ev_idf = emit_identity_ops(sch, "g", identf[:, :], GI)
        ev_eps = sch.op("g", lambda g: g.memset(epsb[:, :], LN_EPS), sem=GI)
        for dj in range(8):
            sch.dma("g", Wo[:, dj, :, :], W_out_r[dj], sem=CONST)
        sch.dma("g", centb[:, :, :],
                centT.rearrange("(dj p) e -> p dj e", p=128), sem=CONST)
        ev_const = sch.dma("g", bob[:, :], b_outT, sem=CONST)

        ev_rld, ev_old, ev_xld = {}, {}, {}

        def loadA(tc, wr=None, wo=None, wx=None):
            slq = tc % 2
            ev_rld[tc] = sch.dma("g", rhbuf[:, slq, :, :],
                RhT.rearrange("(j p) c -> p j c", p=128)[:, :, tc * 512:(tc + 1) * 512],
                waits=[wr], sem=RLD[slq])
            ev_old[tc] = sch.dma("g", ogbuf[:, slq, :, :],
                ogT.rearrange("(j p) c -> p j c", p=128)[:, :, tc * 512:(tc + 1) * 512],
                waits=[wo], sem=OLD[slq])
            ev_xld[tc] = sch.dma("g", xsb[:, slq, :, :],
                xs.rearrange("(tc tt p) d -> tc p tt d", tt=4, p=128)[tc],
                waits=[wx], sem=XLD[slq])

        loadA(0)
        loadA(1)
        mmc = 0
        mm_evac = {}
        pv = {}
        ev_x1st = {}
        for tc in range(NCHUNK):
            slq = tc % 2
            for fj in range(8):
                bank = mmc % 3
                wts = [ev_const, ev_rld[tc], mm_evac.get(mmc - 3)]
                ev = None
                for dj in range(8):
                    ev = sch.op("t", (lambda bank, dj, fj, slq: (lambda t: t.matmul(
                        ps_mm[:, bank, :], Wo[:, dj, fj, :],
                        rhbuf[:, slq, dj, :],
                        start=(dj == 0), stop=(dj == 7))))(bank, dj, fj, slq),
                        waits=wts, sem=TE if dj == 7 else None)
                    wts = []
                e1 = sch.op("s", (lambda bank, fj: (lambda s: s.activation(
                    tstage[:, :], ps_mm[:, bank, :], AF.Identity,
                    bias=bob[:, fj:fj + 1])))(bank, fj),
                    waits=[ev, pv.get("tstage")], sem=SE)
                e2 = sch.op("v", (lambda fj, slq: (lambda v: v.tensor_mul(
                    attnT[:, fj, :], tstage[:, :], ogbuf[:, slq, fj, :])))(fj, slq),
                    waits=[e1, ev_old[tc], pv.get(("attnT", fj))], sem=VE)
                pv["tstage"] = e2
                mm_evac[mmc] = e2
                mmc += 1
                rh_reader = ev
                og_reader = e2
            ev_ln = {}
            for tt in range(4):
                bank = tt % 2
                evt = None
                for fj in range(8):
                    evt = sch.op("t", (lambda bank, fj, tt: (lambda t: t.transpose(
                        ps_tr[:, bank, fj * 128:(fj + 1) * 128],
                        attnT[:, fj, tt * 128:(tt + 1) * 128],
                        identb[:, :])))(bank, fj, tt),
                        waits=[mm_evac[mmc - 8 + fj], ev_id,
                               pv.get(("ps_tr", bank))],
                        sem=TE if fj == 7 else None)
                pv[("attnT", tt)] = evt
                ea = sch.op("v", (lambda bank, tt, slq: (lambda v: v.tensor_add(
                    x1b[:, tt, :], ps_tr[:, bank, :], xsb[:, slq, tt, :])))(bank, tt, slq),
                    waits=[evt, ev_xld[tc], pv.get(("x1b", tt))], sem=VE)
                pv[("ps_tr", bank)] = ea
                eln = emit_ln(sch, x1b[:, tt, :], x1b[:, tt, :], tmpf[:, :],
                              redb[:, 0:3], epsb[:, 0:1],
                              [ea, pv.get("tmpf"), ev_eps], VE, SE)
                pv["tmpf"] = eln
                ev_ln[tt] = eln
                xs_reader = ea
            ev_x1st[tc] = sch.dma("y",
                x1o.rearrange("(tc tt p) d -> tc p tt d", tt=4, p=128)[tc],
                x1b[:, :, :], waits=[ev_ln[tt2] for tt2 in range(4)], sem=X1ST)
            for tt in range(4):
                pv[("x1b", tt)] = ev_x1st[tc]
            ev_xt_done = []
            for tt in range(4):
                i = tc * 4 + tt
                eh = emit_ln(sch, x1b[:, tt, :], tmpf[:, :], lnsq[:, :],
                             redb[:, 3:6], epsb[:, 0:1],
                             [ev_ln[tt], pv.get("tmpf"), pv.get("lnsq")], VE, SE)
                pv["lnsq"] = eh
                eh2 = sch.op("v", (lambda: (lambda v: v.tensor_copy(
                    xhb[:, :], tmpf[:, :])))(),
                    waits=[eh, pv.get("xhb")], sem=VE)
                pv["tmpf"] = eh2
                evt = None
                for fj in range(8):
                    evt = sch.op("t", (lambda fj: (lambda t: t.transpose(
                        ps_tr[:, 0, fj * 128:(fj + 1) * 128],
                        xhb[:, fj * 128:(fj + 1) * 128],
                        identb[:, :])))(fj),
                        waits=[eh2, pv.get(("ps_tr", 0))],
                        sem=TE if fj == 7 else None)
                pv["xhb"] = evt
                ee1 = sch.op("v", (lambda tt: (lambda v: v.tensor_copy(
                    xtb[:, tt, :, :],
                    ps_tr[:, 0, :].rearrange("p (f q) -> p f q", q=128))))(tt),
                    waits=[evt, pv.get(("xtb", tt))], sem=VE)
                pv[("ps_tr", 0)] = ee1
                ev_xt_done.append(ee1)
                ee2 = None
                for rnd in range(2):
                    evt2 = None
                    for k4 in range(4):
                        fj = rnd * 4 + k4
                        evt2 = sch.op("t", (lambda fj, k4, tt: (lambda t: t.transpose(
                            ps_aff[:, k4 * 128:(k4 + 1) * 128],
                            x1b[:, tt, fj * 128:(fj + 1) * 128],
                            identf[:, :])))(fj, k4, tt),
                            waits=[ev_ln[tt], ev_idf, pv.get("ps_aff")],
                            sem=TE if k4 == 3 else None)
                    ee2 = sch.op("v", (lambda rnd: (lambda v: v.tensor_copy(
                        x1tf[:, rnd * 4:(rnd + 1) * 4, :],
                        ps_aff[:, 0:512].rearrange("p (f q) -> p f q", q=128))))(rnd),
                        waits=[evt2, pv.get("x1tb")], sem=VE)
                    pv["ps_aff"] = ee2
                eva = None
                for dj in range(8):
                    eva = sch.op("t", (lambda dj: (lambda t: t.matmul(
                        ps_aff[:, 0:8], x1tf[:, dj, :], centb[:, dj, :],
                        start=(dj == 0), stop=(dj == 7))))(dj),
                        waits=[ee2, ev_const, pv.get("aff"), pv.get("ps_aff")],
                        sem=TE if dj == 7 else None)
                pv["x1tb"] = eva
                pv["ps_aff"] = eva
                ec1 = sch.op("v", (lambda i: (lambda v: v.tensor_copy(
                    afb[:, i, 0:8], ps_aff[:, 0:8])))(i), waits=[eva], sem=VE)
                pv["aff"] = ec1
                pv["ps_aff"] = ec1
                ec2 = sch.op("v", (lambda i: (lambda v: v.reduce_max(
                    afb[:, i, 8:9], afb[:, i, 0:8], axis=AX.X)))(i),
                    waits=[ec1], sem=VE)
                ec3 = sch.op("v", (lambda i: (lambda v: v.tensor_scalar(
                    maskb[:, i, 0:8], afb[:, i, 0:8], afb[:, i, 8:9],
                    None, OP.is_ge)))(i), waits=[ec2], sem=VE)
                ec4 = sch.op("s", (lambda i: (lambda s: s.activation(
                    maskb[:, i, 8:9], afb[:, i, 8:9], AF.Sigmoid)))(i),
                    waits=[ec3], sem=SE)
                pv[("routing", i)] = ec4
            evst = None
            for tt in range(4):
                off = tc * 512 + tt * 128
                evst = sch.dma("y",
                    xhato.rearrange("(f p) c -> p f c", p=128)[:, :, off:off + 128],
                    xtb[:, tt, :, :], waits=[ev_xt_done[tt]], sem=XHST)
            for tt in range(4):
                pv[("xtb", tt)] = evst
            if tc + 2 < NCHUNK:
                loadA(tc + 2, rh_reader, og_reader, xs_reader)

        ev_rt2 = sch.dma("y", masko.rearrange("(i p) e -> p i e", p=128),
                         maskb[:, :, :],
                         waits=[pv[("routing", i)] for i in range(NT)], sem=RTST)
        sch.op("y", lambda y: y.nop(),
               waits=[(X1ST, X1ST.n), (XHST, XHST.n), (RTST, RTST.n)])
        with nc.Block() as blk:
            sch.finalize(blk)
    return nc

def build_l0ffn():
    nc = bass.Bass("TRN2", target_bir_lowering=False, debug=False)
    x1i = nc.dram_tensor("x1", [SH, D], F32, kind="ExternalInput").ap()
    xhati = nc.dram_tensor("xhatT", [D, SH], BF16, kind="ExternalInput").ap()
    maski = nc.dram_tensor("mask", [SH, E0 + 1], F32, kind="ExternalInput").ap()
    ln_gT = nc.dram_tensor("ln_gT", [E0, 128, 8], F32, kind="ExternalInput").ap()
    ln_bT = nc.dram_tensor("ln_bT", [E0, 128, 8], F32, kind="ExternalInput").ap()
    w1T = nc.dram_tensor("w1T", [E0, D, F0], F32, kind="ExternalInput").ap()
    b1T = nc.dram_tensor("b1T", [E0, 128, 32], F32, kind="ExternalInput").ap()
    w2T = nc.dram_tensor("w2T", [E0, F0, D], F32, kind="ExternalInput").ap()
    b2r = nc.dram_tensor("b2r", [E0, D], F32, kind="ExternalInput").ap()
    x2o = nc.dram_tensor("x2", [SH, D], F32, kind="ExternalOutput").ap()
    yd = nc.dram_tensor("yd", [SH, D], BF16, kind="Internal").ap()

    w1T_r = w1T.rearrange("e (dj p) (fj q) -> e dj p fj q", p=128, q=128)
    w2T_r = w2T.rearrange("e (fj p) (dj q) -> e fj p dj q", p=128, q=128)

    with contextlib.ExitStack() as stack:
        sb = lambda *a: stack.enter_context(nc.sbuf_tensor(*a))
        psb = lambda *a: stack.enter_context(nc.psum_tensor(*a))
        w1b = sb("w1b", [128, 8, 32, 128], BF16)
        w2b = sb("w2b", [128, 32, 8, 128], BF16)
        hTf = sb("hTf", [128, 16384], BF16)
        xhcb = sb("xhcb", [128, 2, 8, 512], BF16)
        xeb = sb("xeb", [128, 8, 512], BF16)
        stg = sb("stg", [128, 4, 2, 512], BF16)
        x12 = sb("x12", [128, 1, D], F32)
        maskb = sb("maskb", [128, NT, 9], F32)
        tmpf = sb("tmpf", [128, D], F32)
        redb = sb("redb", [128, 8], F32)
        epsb = sb("epsb", [128, 1], F32)
        lngb = sb("lngb", [128, 8, 8], F32)
        lnbb = sb("lnbb", [128, 8, 8], F32)
        b1b = sb("b1b", [128, 8, 32], F32)
        b2b = sb("b2b", [1, 2, D], BF16)
        ones1r = sb("ones1r", [1, 128], BF16)

        hT = hTf.rearrange("p (f c) -> p f c", c=512)
        ycb = hTf[:, 0:4096].rearrange("p (tt d) -> p tt d", d=D)

        ps_mm = psb("ps_mm", [128, 3, 512], F32)
        ps_m2 = psb("ps_m2", [128, 2, 512], F32)

        sch = Sched(nc, stack)
        CONST = sch.sem("dma")
        W1L = sch.sem("dma")
        W2L = sch.sem("dma")
        XHL = [sch.sem("dma"), sch.sem("dma")]
        YST = [sch.sem("dma"), sch.sem("dma")]
        YLD = sch.sem("dma")
        XL2 = sch.sem("dma")
        X2ST = sch.sem("dma")
        GI = sch.sem("g")
        TE = sch.sem("t")
        VE = sch.sem("v")
        SE = sch.sem("s")

        ev_eps = sch.op("g", lambda g: g.memset(epsb[:, :], LN_EPS), sem=GI)
        ev_o1 = sch.op("g", lambda g: g.memset(ones1r[:, :], 1.0), sem=GI)
        sch.dma("g", lngb[:, :, :], ln_gT.rearrange("e p k -> p e k"), sem=CONST)
        sch.dma("g", lnbb[:, :, :], ln_bT.rearrange("e p k -> p e k"), sem=CONST)
        sch.dma("g", b1b[:, :, :], b1T.rearrange("e p k -> p e k"), sem=CONST)
        ev_const = sch.dma("g", maskb[:, :, :],
                           maski.rearrange("(i p) e -> p i e", p=128), sem=CONST)

        def load_w1(e, wait_ev):
            evw = None
            for dj in range(8):
                evw = sch.dma("g", w1b[:, dj, :, :], w1T_r[e, dj],
                              waits=[wait_ev], sem=W1L)
            return evw

        def load_w2(e, wait_ev):
            evw = None
            for fj8 in range(4):
                evw = sch.dma("g", w2b[:, fj8 * 8:(fj8 + 1) * 8, :, :],
                              w2T_r[e].rearrange("f p dj q -> p f dj q")[:, fj8 * 8:(fj8 + 1) * 8],
                              waits=[wait_ev], sem=W2L)
            return evw

        ev_w1 = load_w1(0, None)
        ev_w2 = load_w2(0, None)
        ev_xh = {}
        pv = {}
        ev_yst = {}
        mm_evac = {}
        mmc = 0
        last_mm1_use = None
        last_mm2_use = None

        def load_xh(tc, wait_ev):
            slq = tc % 2
            ev_xh[tc] = sch.dma("g", xhcb[:, slq, :, :],
                xhati.rearrange("(j p) c -> p j c", p=128)[:, :, tc * 512:(tc + 1) * 512],
                waits=[wait_ev], sem=XHL[slq])

        B2L = sch.sem("dma")
        ev_b2 = {}
        for e in range(E0):
            if e == 0:
                load_xh(0, None)
                load_xh(1, None)
            ev_b2[e] = sch.dma("g", b2b[:, e % 2, :], b2r[e:e + 1, :],
                               waits=[pv.get(("b2", e % 2))], sem=B2L)
            mm1_done = {}
            for tc in range(NCHUNK):
                slq = tc % 2
                exe = None
                for dj in range(8):
                    exe = sch.op("v", (lambda e, dj, slq: (lambda v: v.tensor_scalar(
                        xeb[:, dj, :], xhcb[:, slq, dj, :],
                        lngb[:, e, dj:dj + 1], lnbb[:, e, dj:dj + 1],
                        OP.mult, OP.add)))(e, dj, slq),
                        waits=[ev_xh[tc], ev_const, pv.get("xeb")], sem=VE)
                for fj in range(32):
                    bank = mmc % 3
                    wts = [ev_w1, exe, mm_evac.get(mmc - 3)]
                    ev = None
                    for dj in range(8):
                        ev = sch.op("t", (lambda bank, dj, fj: (lambda t: t.matmul(
                            ps_mm[:, bank, :], w1b[:, dj, fj, :],
                            xeb[:, dj, :],
                            start=(dj == 0), stop=(dj == 7))))(bank, dj, fj),
                            waits=wts, sem=TE if dj == 7 else None)
                        wts = []
                    last_mm1_use = ev
                    mm_evac[mmc] = sch.op("s", (lambda e, bank, fj: (lambda s: s.activation(
                        hT[:, fj, :], ps_mm[:, bank, :], AF.Relu,
                        bias=b1b[:, e, fj:fj + 1])))(e, bank, fj),
                        waits=[ev, pv.get(("hT", fj))], sem=SE)
                    mm1_done[fj] = mm_evac[mmc]
                    mmc += 1
                pv["xeb"] = last_mm1_use
                i0 = tc * 4
                stc = []
                for tsub in range(4):
                    i = i0 + tsub
                    for dch in range(2):
                        bank2 = (tsub * 2 + dch) % 2
                        wts = [pv.get(("ps_m2", bank2)), ev_w2, ev_o1]
                        ev = None
                        for fj in range(32):
                            ev = sch.op("t", (lambda bank2, fj, tsub, dch: (lambda t: t.matmul(
                                ps_m2[:, bank2, :],
                                hT[:, fj, tsub * 128:(tsub + 1) * 128],
                                w2b[:, fj, dch * 4:(dch + 1) * 4, :].rearrange("p a q -> p (a q)"),
                                start=(fj == 0), stop=False)))(bank2, fj, tsub, dch),
                                waits=wts + [mm1_done[fj]], sem=None)
                            wts = []
                        ev = sch.op("t", (lambda bank2, e, dch: (lambda t: t.matmul(
                            ps_m2[:, bank2, :], ones1r[:, :],
                            b2b[:, e % 2, dch * 512:(dch + 1) * 512],
                            start=False, stop=True)))(bank2, e, dch),
                            waits=[ev_b2[e]], sem=TE)
                        last_mm2_use = ev
                        es = sch.op("v", (lambda bank2, i, e, tsub, dch: (lambda v: v.tensor_scalar(
                            stg[:, tsub, dch, :], ps_m2[:, bank2, :],
                            maskb[:, i, e:e + 1], None, OP.mult)))(bank2, i, e, tsub, dch),
                            waits=[ev, ev_yst.get((e, tc - 1)) if tc > 0
                                   else ev_yst.get((e - 1, NCHUNK - 1))],
                            sem=VE)
                        pv[("ps_m2", bank2)] = es
                        stc.append(es)
                kw = {} if e == 0 else {"accum_op": OP.add}
                ev_yst[(e, tc)] = sch.dma("g",
                    yd.rearrange("(tc ts p) (dc c) -> tc p ts dc c",
                                 ts=4, p=128, c=512)[tc],
                    stg[:, :, :, :], waits=stc, sem=YST[slq], **kw)
                nxt = tc + 2
                if nxt < NCHUNK:
                    load_xh(nxt, pv.get("xeb"))
                elif e + 1 < E0:
                    load_xh(nxt - NCHUNK, pv.get("xeb"))
            pv[("b2", e % 2)] = last_mm2_use
            if e + 1 < E0:
                ev_w1 = load_w1(e + 1, last_mm1_use)
                ev_w2 = load_w2(e + 1, last_mm2_use)

        for tc in range(NCHUNK):
            eyl = sch.dma("g", ycb[:, :, :],
                yd.rearrange("(tc tt p) d -> tc p tt d", tt=4, p=128)[tc],
                waits=[ev_yst[(E0 - 1, tc)], last_mm2_use if tc == 0 else None,
                       pv.get("ycb")], sem=YLD)
            ecs = []
            for tt in range(4):
                i = tc * 4 + tt
                exl = sch.dma("y", x12[:, 0, :],
                    x1i[(tc * 4 + tt) * 128:(tc * 4 + tt + 1) * 128, :],
                    waits=[pv.get("x12")], sem=XL2)
                e1 = sch.op("v", (lambda tt, i: (lambda v: v.scalar_tensor_tensor(
                    tmpf[:, :], ycb[:, tt, :], maskb[:, i, 8:9],
                    x12[:, 0, :], OP.mult, OP.add)))(tt, i),
                    waits=[eyl, exl, pv.get("tmpf")], sem=VE)
                e1b = sch.op("v", lambda v: v.tensor_tensor(
                    tmpf[:, :], tmpf[:, :], x12[:, 0, :], OP.add),
                    waits=[e1], sem=VE)
                eln = emit_ln(sch, tmpf[:, :], tmpf[:, :], x12[:, 0, :],
                              redb[:, 0:3], epsb[:, 0:1],
                              [e1b, ev_eps], VE, SE)
                pv["x12"] = eln
                est = sch.dma("y", x2o[i * 128:(i + 1) * 128, :],
                              tmpf[:, :], waits=[eln], sem=X2ST)
                pv["tmpf"] = est
                ecs.append(est)
            pv["ycb"] = ecs[-1]

        sch.op("y", lambda y: y.nop(), waits=[(X2ST, X2ST.n)])
        with nc.Block() as blk:
            sch.finalize(blk)
    return nc



def stage_l0_weights(inputs):
    W_in = f32c(inputs["L0_attn_W_in"])
    W_ctrl = np.asarray(inputs["L0_attn_W_ctrl"], dtype=np.float32)
    b_ctrl = np.asarray(inputs["L0_attn_b_ctrl"], dtype=np.float32)
    Wc = np.zeros((D, 1280), dtype=np.float32)
    Wc[:, 0:1024] = W_ctrl[:, 8:]
    Wc[:, 1024:1028] = W_ctrl[:, 0:4]
    Wc[:, 1152:1156] = W_ctrl[:, 4:8]
    bcT = np.zeros((128, 10), dtype=np.float32)
    for fj in range(8):
        bcT[:, fj] = b_ctrl[8 + fj * 128: 8 + (fj + 1) * 128]
    bcT[0:4, 8] = b_ctrl[0:4]
    bcT[0:4, 9] = b_ctrl[4:8]
    cwall = np.zeros((128, 24, 4), dtype=np.float32)
    for kind, w in enumerate((inputs["L0_attn_w_q"], inputs["L0_attn_w_k"],
                              inputs["L0_attn_w_v"])):
        w = np.asarray(w, dtype=np.float32)[:, 0, :]
        for dj in range(8):
            cwall[:, kind * 8 + dj, :] = w[dj * 128:(dj + 1) * 128, :]
    rampT = np.zeros((128, 8), dtype=np.float32)
    rl = np.asarray(inputs["L0_attn_ramp_logit"], dtype=np.float32)
    for dj in range(8):
        rampT[:, dj] = rl[dj * 128:(dj + 1) * 128]
    W_out = f32c(inputs["L0_attn_W_out"])
    b_out = np.asarray(inputs["L0_attn_b_out"], dtype=np.float32)
    b_outT = np.zeros((128, 8), dtype=np.float32)
    for fj in range(8):
        b_outT[:, fj] = b_out[fj * 128:(fj + 1) * 128]
    centT = f32c(np.asarray(inputs["L0_ffn_centroids"], np.float32).T)
    ln_g = np.asarray(inputs["L0_ffn_ln_g"], np.float32)
    ln_b = np.asarray(inputs["L0_ffn_ln_b"], np.float32)
    ln_gT = f32c(ln_g.reshape(E0, 8, 128).transpose(0, 2, 1))
    ln_bT = f32c(ln_b.reshape(E0, 8, 128).transpose(0, 2, 1))
    w1T = f32c(np.asarray(inputs["L0_ffn_w1"], np.float32).transpose(0, 2, 1))
    b1 = np.asarray(inputs["L0_ffn_b1"], np.float32)
    b1T = f32c(b1.reshape(E0, 32, 128).transpose(0, 2, 1))
    w2T = f32c(np.asarray(inputs["L0_ffn_w2"], np.float32).transpose(0, 2, 1))
    b2r = f32c(inputs["L0_ffn_b2"])
    return {"W_in": W_in, "W_ctrl": f32c(Wc), "b_ctrlT": f32c(bcT),
            "cwall": f32c(cwall), "rampT": f32c(rampT), "W_out": W_out,
            "b_outT": f32c(b_outT), "centT": centT, "ln_gT": ln_gT,
            "ln_bT": ln_bT, "w1T": w1T, "b1T": b1T, "w2T": w2T, "b2r": b2r}


def stage_l1_weights(inputs):
    Wq = f32c(inputs["L1_attn_Wq"])
    Wk = f32c(inputs["L1_attn_Wk"])
    Wv = f32c(inputs["L1_attn_Wv"])
    bq = np.asarray(inputs["L1_attn_bq"], np.float32)
    bk = np.asarray(inputs["L1_attn_bk"], np.float32)
    bqT = f32c(bq.reshape(8, 128).T)
    bkT = f32c(bk.reshape(8, 128).T)
    bvr = f32c(np.asarray(inputs["L1_attn_bv"], np.float32)[None, :])
    wgT = f32c(np.asarray(inputs["L1_ffn_w_gate"], np.float32).T)
    w1eT = f32c(np.asarray(inputs["L1_ffn_w1_e"], np.float32).transpose(0, 2, 1))
    w3eT = f32c(np.asarray(inputs["L1_ffn_w3_e"], np.float32).transpose(0, 2, 1))
    w2eT = f32c(np.asarray(inputs["L1_ffn_w2_e"], np.float32).transpose(0, 2, 1))
    w1sT = f32c(np.asarray(inputs["L1_ffn_w1_s"], np.float32).T)
    w3sT = f32c(np.asarray(inputs["L1_ffn_w3_s"], np.float32).T)
    w2sT = f32c(np.asarray(inputs["L1_ffn_w2_s"], np.float32).T)
    return {"Wq": Wq, "Wk": Wk, "Wv": Wv, "bqT": bqT, "bkT": bkT, "bvr": bvr,
            "wgT": wgT, "w1eT": w1eT, "w3eT": w3eT, "w2eT": w2eT,
            "w1sT": w1sT, "w3sT": w3sT, "w2sT": w2sT}


def run_l0(inputs, W):
    x = np.asarray(inputs["L0_attn_x"], dtype=np.float32)
    nc1 = build_pre()
    im1 = []
    xsl = {}
    for core in range(N_CORES):
        b, half = core // 2, core % 2
        t0 = half * SH
        xsl[core] = f32c(x[b, t0:t0 + SH, :])
        halo = (np.zeros((3, D), dtype=np.float32) if t0 == 0
                else f32c(x[b, t0 - 3:t0, :]))
        im1.append({"xs": xsl[core], "xhalo": halo, "W_in": W["W_in"],
                    "W_ctrl": W["W_ctrl"], "b_ctrlT": W["b_ctrlT"],
                    "cwall": W["cwall"], "rampT": W["rampT"]})
    r1 = run_bass_kernel_spmd(nc1, im1, core_ids=list(range(N_CORES))).results
    nc2 = build_scan()
    im2 = []
    for core in range(N_CORES):
        b = core // 2
        hsel = slice(0, 512) if core % 2 == 0 else slice(512, 1024)
        qf = np.concatenate([np.asarray(r1[2 * b]["qhT"]),
                             np.asarray(r1[2 * b + 1]["qhT"])], axis=1)
        kf = np.concatenate([np.asarray(r1[2 * b]["khT"]),
                             np.asarray(r1[2 * b + 1]["khT"])], axis=1)
        vf = np.concatenate([np.asarray(r1[2 * b]["V"]),
                             np.asarray(r1[2 * b + 1]["V"])], axis=0)
        af = np.concatenate([np.asarray(r1[2 * b]["alpha"]),
                             np.asarray(r1[2 * b + 1]["alpha"])], axis=1)
        bf_ = np.concatenate([np.asarray(r1[2 * b]["beta"]),
                              np.asarray(r1[2 * b + 1]["beta"])], axis=1)
        h2 = slice(0, 2) if core % 2 == 0 else slice(2, 4)
        im2.append({"qhT": np.ascontiguousarray(qf[hsel]),
                    "khT": np.ascontiguousarray(kf[hsel]),
                    "V": np.ascontiguousarray(vf[:, hsel]),
                    "alpha": np.ascontiguousarray(af[h2]),
                    "beta": np.ascontiguousarray(bf_[h2])})
    r2 = run_bass_kernel_spmd(nc2, im2, core_ids=list(range(N_CORES))).results
    nc3 = build_l0proj()
    im3 = []
    for core in range(N_CORES):
        b, half = core // 2, core % 2
        t0 = half * SH
        rh_full = np.concatenate([np.asarray(r2[2 * b]["RhT"]),
                                  np.asarray(r2[2 * b + 1]["RhT"])], axis=0)
        im3.append({"RhT": np.ascontiguousarray(rh_full[:, t0:t0 + SH]),
                    "ogT": np.asarray(r1[core]["ogT"]),
                    "xs": xsl[core], "W_out": W["W_out"],
                    "b_outT": W["b_outT"], "centT": W["centT"]})
    r3 = run_bass_kernel_spmd(nc3, im3, core_ids=list(range(N_CORES))).results
    nc4 = build_l0ffn()
    im4 = []
    for core in range(N_CORES):
        im4.append({"x1": np.asarray(r3[core]["x1"]),
                    "xhatT": np.asarray(r3[core]["xhatT"]),
                    "mask": np.asarray(r3[core]["mask"]),
                    "ln_gT": W["ln_gT"], "ln_bT": W["ln_bT"],
                    "w1T": W["w1T"], "b1T": W["b1T"], "w2T": W["w2T"],
                    "b2r": W["b2r"]})
    r4 = run_bass_kernel_spmd(nc4, im4, core_ids=list(range(N_CORES))).results
    return [np.asarray(r4[c]["x2"]) for c in range(N_CORES)], r1, r2, r3

def build_l1attn():
    nc = bass.Bass("TRN2", target_bir_lowering=False, debug=False)
    x2kv = nc.dram_tensor("x2kv", [SH, D], F32, kind="ExternalInput").ap()
    x2q1 = nc.dram_tensor("x2q1", [SH, D], F32, kind="ExternalInput").ap()
    x2q2 = nc.dram_tensor("x2q2", [SH, D], F32, kind="ExternalInput").ap()
    Wq = nc.dram_tensor("Wq", [D, D], F32, kind="ExternalInput").ap()
    Wk = nc.dram_tensor("Wk", [D, D], F32, kind="ExternalInput").ap()
    Wv = nc.dram_tensor("Wv", [D, D], F32, kind="ExternalInput").ap()
    bqT = nc.dram_tensor("bqT", [128, 8], F32, kind="ExternalInput").ap()
    bkT = nc.dram_tensor("bkT", [128, 8], F32, kind="ExternalInput").ap()
    bvr = nc.dram_tensor("bvr", [1, D], F32, kind="ExternalInput").ap()
    pctxT = nc.dram_tensor("pctxT", [D, S], F32, kind="ExternalOutput").ap()
    qTd = nc.dram_tensor("qTd", [D, S], BF16, kind="Internal").ap()

    Wq_r = Wq.rearrange("(dj p) (fj q) -> dj p fj q", p=128, q=128)
    Wk_r = Wk.rearrange("(dj p) (fj q) -> dj p fj q", p=128, q=128)
    Wv_r = Wv.rearrange("(dj p) (fj q) -> dj p fj q", p=128, q=128)
    SCALE = 1.0 / 32.0

    with contextlib.ExitStack() as stack:
        sb = lambda *a: stack.enter_context(nc.sbuf_tensor(*a))
        psb = lambda *a: stack.enter_context(nc.psum_tensor(*a))
        identb = sb("identb", [128, 128], BF16)
        Wqb = sb("Wqb", [128, 8, 8, 128], BF16)
        Wkb = sb("Wkb", [128, 8, 8, 128], BF16)
        Wvb = sb("Wvb", [128, 8, 8, 128], BF16)
        bqb = sb("bqb", [128, 8], F32)
        bkb = sb("bkb", [128, 8], F32)
        bvb = sb("bvb", [1, D], BF16)
        ones1r = sb("ones1r", [1, 128], BF16)
        xcb = sb("xcb", [128, 4, D], F32)
        xbb = sb("xbb", [128, D], BF16)
        xtc = sb("xtc", [128, 8, 512], BF16)
        KTs = sb("KTs", [128, 8, SH], BF16)
        Vs = sb("Vs", [128, NT, D], BF16)
        qstg = KTs[:, :, 7 * 512:8 * 512]
        qTb = Wqb.rearrange("p a b c -> p (a b c)").rearrange(
            "p (s f c) -> p s f c", s=2, c=512)
        Eb = Wkb.rearrange("p a b c -> p (a b c)").rearrange(
            "p (s f c) -> p s f c", s=2, c=512)
        Zb = sb("Zb", [128, 512], F32)

        ps_mm = psb("ps_mm", [128, 2, 512], F32)
        ps_ctx = psb("ps_ctx", [128, 5, 512], F32)
        ps_trb = psb("ps_trb", [128, 1024], BF16)

        sch = Sched(nc, stack)
        CONST = sch.sem("dma")
        XL = [sch.sem("dma"), sch.sem("dma")]
        QST = sch.sem("dma")
        QLD = [sch.sem("dma"), sch.sem("dma")]
        CST = sch.sem("dma")
        GI = sch.sem("g")
        TE = sch.sem("t")
        VE = sch.sem("v")
        SE = sch.sem("s")

        ev_id = emit_identity_ops(sch, "g", identb[:, :], GI)
        ev_o1 = sch.op("g", lambda g: g.memset(ones1r[:, :], 1.0), sem=GI)
        for dj in range(8):
            sch.dma("g", Wqb[:, dj, :, :], Wq_r[dj], sem=CONST)
            sch.dma("g", Wkb[:, dj, :, :], Wk_r[dj], sem=CONST)
            sch.dma("g", Wvb[:, dj, :, :], Wv_r[dj], sem=CONST)
        sch.dma("g", bqb[:, :], bqT, sem=CONST)
        sch.dma("g", bkb[:, :], bkT, sem=CONST)
        ev_const = sch.dma("g", bvb[:, :], bvr, sem=CONST)

        pv = {}
        mmc = 0
        mm_evac = {}

        ev_q_done = {}

        def proj_chunk(gc, src, make_kv, kvc):
            nonlocal mmc
            slq = gc % 2
            evx = sch.dma("g", xcb[:, :, :],
                src.rearrange("(tc tt p) d -> tc p tt d", tt=4, p=128)[gc % 8],
                waits=[pv.get("xcb")], sem=XL[slq])
            evts = []
            for tt in range(4):
                eb = sch.op("v", (lambda tt: (lambda v: v.tensor_copy(
                    xbb[:, :], xcb[:, tt, :])))(tt),
                    waits=[evx, pv.get("xbb")], sem=VE)
                evt = None
                for fj in range(8):
                    evt = sch.op("t", (lambda fj: (lambda t: t.transpose(
                        ps_trb[:, fj * 128:(fj + 1) * 128],
                        xbb[:, fj * 128:(fj + 1) * 128],
                        identb[:, :])))(fj),
                        waits=[eb, ev_id, pv.get("ps_trb")],
                        sem=TE if fj == 7 else None)
                pv["xbb"] = evt
                ee = sch.op("v", (lambda tt: (lambda v: v.tensor_copy(
                    xtc[:, :, tt * 128:(tt + 1) * 128],
                    ps_trb.rearrange("p (f q) -> p f q", q=128))))(tt),
                    waits=[evt, pv.get("xtc")], sem=VE)
                pv["ps_trb"] = ee
                evts.append(ee)
            pv["xcb"] = evts[-1]
            for fj in range(8):
                bank = mmc % 2
                wts = [ev_const, mm_evac.get(mmc - 2)] + evts
                ev = None
                for dj in range(8):
                    ev = sch.op("t", (lambda bank, dj, fj: (lambda t: t.matmul(
                        ps_mm[:, bank, :], Wqb[:, dj, fj, :], xtc[:, dj, :],
                        start=(dj == 0), stop=(dj == 7))))(bank, dj, fj),
                        waits=wts, sem=TE if dj == 7 else None)
                    wts = []
                eq = sch.op("s", (lambda bank, fj: (lambda s: s.activation(
                    qstg[:, fj, :], ps_mm[:, bank, :], AF.Identity,
                    bias=bqb[:, fj:fj + 1])))(bank, fj),
                    waits=[ev, pv.get(("qstg", fj))], sem=SE)
                pv[("qstg", fj)] = eq
                mm_evac[mmc] = eq
                mmc += 1
            ev_q_done[gc] = sch.dma("y",
                qTd.rearrange("(j p) c -> p j c", p=128)[:, :, gc * 512:(gc + 1) * 512],
                qstg[:, :, :],
                waits=[pv[("qstg", fj2)] for fj2 in range(8)], sem=QST)
            for fj in range(8):
                pv[("qstg", fj)] = ev_q_done[gc]
            if make_kv:
                for fj in range(8):
                    bank = mmc % 2
                    wts = [mm_evac.get(mmc - 2)]
                    ev = None
                    for dj in range(8):
                        ev = sch.op("t", (lambda bank, dj, fj: (lambda t: t.matmul(
                            ps_mm[:, bank, :], Wkb[:, dj, fj, :], xtc[:, dj, :],
                            start=(dj == 0), stop=(dj == 7))))(bank, dj, fj),
                            waits=wts, sem=TE if dj == 7 else None)
                        wts = []
                    ek = sch.op("s", (lambda bank, fj, kvc: (lambda s: s.activation(
                        KTs[:, fj, kvc * 512:(kvc + 1) * 512], ps_mm[:, bank, :],
                        AF.Identity, bias=bkb[:, fj:fj + 1])))(bank, fj, kvc),
                        waits=[ev], sem=SE)
                    mm_evac[mmc] = ek
                    mmc += 1
                for tt in range(4):
                    for dch in range(2):
                        bank = mmc % 2
                        wts = [mm_evac.get(mmc - 2), ev_o1]
                        ev = None
                        for dj in range(8):
                            ev = sch.op("t", (lambda bank, dj, tt, dch: (lambda t: t.matmul(
                                ps_mm[:, bank, :], xtc[:, dj, tt * 128:(tt + 1) * 128],
                                Wvb[:, dj, dch * 4:(dch + 1) * 4, :].rearrange("p a q -> p (a q)"),
                                start=(dj == 0), stop=False)))(bank, dj, tt, dch),
                                waits=wts, sem=None)
                            wts = []
                        ev = sch.op("t", (lambda bank, dch: (lambda t: t.matmul(
                            ps_mm[:, bank, :], ones1r[:, :],
                            bvb[:, dch * 512:(dch + 1) * 512],
                            start=False, stop=True)))(bank, dch),
                            waits=[], sem=TE)
                        evv = sch.op("v", (lambda bank, tt, dch, kvc: (lambda v: v.tensor_copy(
                            Vs[:, kvc * 4 + tt, dch * 512:(dch + 1) * 512],
                            ps_mm[:, bank, :])))(bank, tt, dch, kvc),
                            waits=[ev], sem=VE)
                        mm_evac[mmc] = evv
                        mmc += 1

        for gc in range(16):
            src = x2q1 if gc < 8 else x2q2
            proj_chunk(gc, src, False, 0)
        for kvc in range(8):
            proj_chunk_src = x2kv
            slq = kvc % 2
            evx = sch.dma("g", xcb[:, :, :],
                x2kv.rearrange("(tc tt p) d -> tc p tt d", tt=4, p=128)[kvc],
                waits=[pv.get("xcb")], sem=XL[slq])
            evts = []
            for tt in range(4):
                eb = sch.op("v", (lambda tt: (lambda v: v.tensor_copy(
                    xbb[:, :], xcb[:, tt, :])))(tt),
                    waits=[evx, pv.get("xbb")], sem=VE)
                evt = None
                for fj in range(8):
                    evt = sch.op("t", (lambda fj: (lambda t: t.transpose(
                        ps_trb[:, fj * 128:(fj + 1) * 128],
                        xbb[:, fj * 128:(fj + 1) * 128],
                        identb[:, :])))(fj),
                        waits=[eb, ev_id, pv.get("ps_trb")],
                        sem=TE if fj == 7 else None)
                pv["xbb"] = evt
                ee = sch.op("v", (lambda tt: (lambda v: v.tensor_copy(
                    xtc[:, :, tt * 128:(tt + 1) * 128],
                    ps_trb.rearrange("p (f q) -> p f q", q=128))))(tt),
                    waits=[evt, pv.get("xtc")], sem=VE)
                pv["ps_trb"] = ee
                evts.append(ee)
            pv["xcb"] = evts[-1]
            for fj in range(8):
                bank = mmc % 2
                wts = [mm_evac.get(mmc - 2)] + evts
                ev = None
                for dj in range(8):
                    ev = sch.op("t", (lambda bank, dj, fj: (lambda t: t.matmul(
                        ps_mm[:, bank, :], Wkb[:, dj, fj, :], xtc[:, dj, :],
                        start=(dj == 0), stop=(dj == 7))))(bank, dj, fj),
                        waits=wts, sem=TE if dj == 7 else None)
                    wts = []
                ek = sch.op("s", (lambda bank, fj, kvc: (lambda s: s.activation(
                    KTs[:, fj, kvc * 512:(kvc + 1) * 512], ps_mm[:, bank, :],
                    AF.Identity, bias=bkb[:, fj:fj + 1])))(bank, fj, kvc),
                    waits=[ev] + ([ev_q_done[15]] if kvc == 7 else []), sem=SE)
                mm_evac[mmc] = ek
                mmc += 1
            for tt in range(4):
                for dch in range(2):
                    bank = mmc % 2
                    wts = [mm_evac.get(mmc - 2), ev_o1]
                    ev = None
                    for dj in range(8):
                        ev = sch.op("t", (lambda bank, dj, tt, dch: (lambda t: t.matmul(
                            ps_mm[:, bank, :], xtc[:, dj, tt * 128:(tt + 1) * 128],
                            Wvb[:, dj, dch * 4:(dch + 1) * 4, :].rearrange("p a q -> p (a q)"),
                            start=(dj == 0), stop=False)))(bank, dj, tt, dch),
                            waits=wts, sem=None)
                        wts = []
                    ev = sch.op("t", (lambda bank, dch: (lambda t: t.matmul(
                        ps_mm[:, bank, :], ones1r[:, :],
                        bvb[:, dch * 512:(dch + 1) * 512],
                        start=False, stop=True)))(bank, dch),
                        waits=[], sem=TE)
                    evv = sch.op("v", (lambda bank, tt, dch, kvc: (lambda v: v.tensor_copy(
                        Vs[:, kvc * 4 + tt, dch * 512:(dch + 1) * 512],
                        ps_mm[:, bank, :])))(bank, tt, dch, kvc),
                        waits=[ev], sem=VE)
                    mm_evac[mmc] = evv
                    mmc += 1
        ev_kv_done = mm_evac[mmc - 1]
        Zi = xcb[:, 0, 0:512]
        ctx67 = xcb[:, 1:4, 0:512]
        pv["Zi"] = ev_kv_done
        pv["ctx67w"] = ev_kv_done

        ev_qld = {}

        def load_q(nc_, wait_ev):
            slq = nc_ % 2
            ev_qld[nc_] = sch.dma("g", qTb[:, slq, :, :],
                qTd.rearrange("(j p) c -> p j c", p=128)[:, :, nc_ * 512:(nc_ + 1) * 512],
                waits=[wait_ev, ev_q_done[nc_]], sem=QLD[slq])

        load_q(0, ev_q_done[15])
        load_q(1, ev_q_done[15])
        for nc_ in range(16):
            slq = nc_ % 2
            ectx_first = {}
            for m in range(NT):
                msl = m % 2
                for g in range(8):
                    bank = mmc % 2
                    ev = sch.op("t", (lambda bank, g, m, slq: (lambda t: t.matmul(
                        ps_mm[:, bank, :], KTs[:, g, m * 128:(m + 1) * 128],
                        qTb[:, slq, g, :],
                        start=True, stop=True)))(bank, g, m, slq),
                        waits=[ev_kv_done, ev_qld[nc_], mm_evac.get(mmc - 2)],
                        sem=TE)
                    q_reader = ev
                    ee = sch.op("s", (lambda bank, g, msl: (lambda s: s.activation(
                        Eb[:, msl, g, :], ps_mm[:, bank, :], AF.Exp,
                        scale=SCALE)))(bank, g, msl),
                        waits=[ev, pv.get(("E", msl, g))], sem=SE)
                    mm_evac[mmc] = ee
                    mmc += 1
                    pv[("Eexp", msl, g)] = ee
                ez1 = sch.op("v", (lambda msl: (lambda v: v.tensor_add(
                    Zb[:, :], Eb[:, msl, 0, :], Eb[:, msl, 1, :])))(msl),
                    waits=[pv[("Eexp", msl, 0)], pv[("Eexp", msl, 1)],
                           pv.get("Zb")], sem=VE)
                for g in range(2, 8):
                    ez1 = sch.op("v", (lambda msl, g: (lambda v: v.tensor_add(
                        Zb[:, :], Zb[:, :], Eb[:, msl, g, :])))(msl, g),
                        waits=[ez1, pv[("Eexp", msl, g)]], sem=VE)
                ez2 = sch.op("v", (lambda: (lambda v: v.reciprocal(
                    Zi, Zb[:, :])))(), waits=[ez1, pv.get("Zi")], sem=VE)
                pv["Zb"] = ez2
                evps = []
                for g in range(8):
                    ep = sch.op("v", (lambda msl, g: (lambda v: v.tensor_mul(
                        Eb[:, msl, g, :], Eb[:, msl, g, :], Zi)))(msl, g),
                        waits=[ez2, pv.get(("P", msl, g))], sem=VE)
                    pv[("E", msl, g)] = ep
                    evps.append(ep)
                pv["Zi"] = evps[-1]
                for g in range(5):
                    ev = sch.op("t", (lambda g, m, msl: (lambda t: t.matmul(
                        ps_ctx[:, g, :], Vs[:, m, g * 128:(g + 1) * 128],
                        Eb[:, msl, g, :],
                        start=(m == 0), stop=(m == NT - 1))))(g, m, msl),
                        waits=[evps[g], pv.get(("ctxe", g))], sem=TE)
                    pv[("P", msl, g)] = ev
                    if m == NT - 1:
                        ectx_first[g] = ev
                for g in (5, 6, 7):
                    bank = mmc % 2
                    ev = sch.op("t", (lambda bank, g, m, msl: (lambda t: t.matmul(
                        ps_mm[:, bank, :], Vs[:, m, g * 128:(g + 1) * 128],
                        Eb[:, msl, g, :],
                        start=True, stop=True)))(bank, g, m, msl),
                        waits=[evps[g], mm_evac.get(mmc - 2)], sem=TE)
                    pv[("P", msl, g)] = ev
                    if m == 0:
                        ea = sch.op("v", (lambda bank, g: (lambda v: v.tensor_copy(
                            ctx67[:, g - 5, :], ps_mm[:, bank, :])))(bank, g),
                            waits=[ev, pv.get("ctx67w")], sem=VE)
                    else:
                        ea = sch.op("v", (lambda bank, g: (lambda v: v.tensor_add(
                            ctx67[:, g - 5, :], ctx67[:, g - 5, :],
                            ps_mm[:, bank, :])))(bank, g),
                            waits=[ev], sem=VE)
                    mm_evac[mmc] = ea
                    mmc += 1
                    if m == NT - 1:
                        ectx_first[g] = ea
            est_l = []
            for g in range(5):
                ec = sch.op("v", (lambda g: (lambda v: v.tensor_copy(
                    Zb[:, :], ps_ctx[:, g, :])))(g),
                    waits=[ectx_first[g], pv.get("Zb"), pv.get("cstg")], sem=VE)
                pv[("ctxe", g)] = ec
                es = sch.dma("y",
                    pctxT[g * 128:(g + 1) * 128, nc_ * 512:(nc_ + 1) * 512],
                    Zb[:, :], waits=[ec], sem=CST)
                pv["cstg"] = es
                pv["Zb"] = es
                est_l.append(es)
            for g in (5, 6, 7):
                es = sch.dma("y",
                    pctxT[g * 128:(g + 1) * 128, nc_ * 512:(nc_ + 1) * 512],
                    ctx67[:, g - 5, :], waits=[ectx_first[g]], sem=CST)
                est_l.append(es)
            pv["ctx67w"] = est_l[-1]
            if nc_ + 2 < 16:
                load_q(nc_ + 2, q_reader)

        sch.op("y", lambda y: y.nop(), waits=[(CST, CST.n)])
        with nc.Block() as blk:
            sch.finalize(blk)
    return nc

def build_l1tail():
    nc = bass.Bass("TRN2", target_bir_lowering=False, debug=False)
    pctxA = nc.dram_tensor("pctxA", [D, SH], F32, kind="ExternalInput").ap()
    pctxB = nc.dram_tensor("pctxB", [D, SH], F32, kind="ExternalInput").ap()
    x2i = nc.dram_tensor("x2", [SH, D], F32, kind="ExternalInput").ap()
    wgT = nc.dram_tensor("wgT", [D, E1], F32, kind="ExternalInput").ap()
    w1eT = nc.dram_tensor("w1eT", [E1, D, F1], F32, kind="ExternalInput").ap()
    w3eT = nc.dram_tensor("w3eT", [E1, D, F1], F32, kind="ExternalInput").ap()
    w2eT = nc.dram_tensor("w2eT", [E1, F1, D], F32, kind="ExternalInput").ap()
    w1sT = nc.dram_tensor("w1sT", [D, FS], F32, kind="ExternalInput").ap()
    w3sT = nc.dram_tensor("w3sT", [D, FS], F32, kind="ExternalInput").ap()
    w2sT = nc.dram_tensor("w2sT", [FS, D], F32, kind="ExternalInput").ap()
    outo = nc.dram_tensor("out", [SH, D], F32, kind="ExternalOutput").ap()
    x3d = nc.dram_tensor("x3d", [SH, D], F32, kind="Internal").ap()
    x3Td = nc.dram_tensor("x3Td", [D, SH], BF16, kind="Internal").ap()
    y1d = nc.dram_tensor("y1d", [SH, D], BF16, kind="Internal").ap()

    w1eT_r = w1eT.rearrange("e (dj p) f -> e dj p f", p=128)
    w3eT_r = w3eT.rearrange("e (dj p) f -> e dj p f", p=128)
    w2eT_r = w2eT.rearrange("e (fj p) d -> e fj p d", p=128)
    w1sT_r = w1sT.rearrange("(dj p) f -> dj p f", p=128)
    w3sT_r = w3sT.rearrange("(dj p) f -> dj p f", p=128)
    w2sT_r = w2sT.rearrange("(fj p) d -> fj p d", p=128)
    BIG = 1.0e30

    with contextlib.ExitStack() as stack:
        sb = lambda *a: stack.enter_context(nc.sbuf_tensor(*a))
        psb = lambda *a: stack.enter_context(nc.psum_tensor(*a))
        identb = sb("identb", [128, 128], BF16)
        epsb = sb("epsb", [128, 1], F32)
        onesb = sb("onesb", [128, 4], F32)
        ctxb = sb("ctxb", [128, 2, 8, 512], F32)
        ctf = sb("ctf", [128, 8, 512], F32)
        ctbf = sb("ctbf", [128, 8, 512], BF16)
        x2cb = sb("x2cb", [128, 4, D], F32)
        x3b = sb("x3b", [128, 4, D], F32)
        x3c = sb("x3c", [128, D], BF16)
        x3tb = sb("x3tb", [128, 8, 128], BF16)
        tmpf = sb("tmpf", [128, D], F32)
        redb = sb("redb", [128, 8], F32)
        gsc = sb("gsc", [128, NT, 16], F32)
        comb = sb("comb", [128, NT, 16], F32)
        gtmp = sb("gtmp", [128, 16], F32)
        g2b = sb("g2b", [128, 8], F32)
        wgb = sb("wgb", [128, 8, 16], BF16)
        w1b = sb("w1b", [128, 2, 8, 512], BF16)
        w3b = sb("w3b", [128, 2, 8, 512], BF16)
        w2b = sb("w2b", [128, 2, 4, 1024], BF16)
        xtcb = sb("xtcb", [128, 2, 8, 128], BF16)
        gsb = sb("gsb", [128, 512], BF16)
        gub = sb("gub", [128, 512], BF16)
        guT = sb("guT", [128, 4, 128], BF16)
        y1a = sb("y1a", [128, D], BF16)

        ps_mm = psb("ps_mm", [128, 3, 512], F32)
        ps_y = psb("ps_y", [128, 2, 512], F32)
        ps_tr = psb("ps_tr", [128, 2, 1024], BF16)
        ps_g = psb("ps_g", [128, 512], F32)

        sch = Sched(nc, stack)
        CONST = sch.sem("dma")
        CXL = [sch.sem("dma"), sch.sem("dma")]
        X2L = [sch.sem("dma"), sch.sem("dma")]
        X3ST = sch.sem("dma")
        XTST = sch.sem("dma")
        XTL = [sch.sem("dma"), sch.sem("dma")]
        WEL = sch.sem("dma")
        X3L = sch.sem("dma")
        YDST = [sch.sem("dma"), sch.sem("dma")]
        YDL = sch.sem("dma")
        OST = sch.sem("dma")
        GI = sch.sem("g")
        TE = sch.sem("t")
        VE = sch.sem("v")
        SE = sch.sem("s")

        ev_id = emit_identity_ops(sch, "g", identb[:, :], GI)
        ev_eps = sch.op("g", lambda g: g.memset(epsb[:, :], LN_EPS), sem=GI)
        ev_on = sch.op("g", lambda g: g.memset(onesb[:, :], 1.0), sem=GI)
        ev_wg = sch.dma("g", wgb[:, :, :],
                        wgT.rearrange("(dj p) e -> p dj e", p=128), sem=CONST)

        pv = {}
        mmc = 0
        mm_evac = {}

        ev_x3st = {}
        ev_xtst = {}
        for tc in range(NCHUNK):
            slq = tc % 2
            eca = sch.dma("g", ctxb[:, 0, :, :],
                pctxA.rearrange("(j p) c -> p j c", p=128)[:, :, tc * 512:(tc + 1) * 512],
                waits=[pv.get("ctxA")], sem=CXL[0])
            ecb = sch.dma("g", ctxb[:, 1, :, :],
                pctxB.rearrange("(j p) c -> p j c", p=128)[:, :, tc * 512:(tc + 1) * 512],
                waits=[pv.get("ctxB")], sem=CXL[1])
            ex2 = sch.dma("g", x2cb[:, :, :],
                x2i.rearrange("(tc tt p) d -> tc p tt d", tt=4, p=128)[tc],
                waits=[pv.get("x2cb")], sem=X2L[slq])
            e1 = sch.op("v", (lambda: (lambda v: v.tensor_add(
                ctf[:, :, :], ctxb[:, 0, :, :], ctxb[:, 1, :, :])))(),
                waits=[eca, ecb, pv.get("ctf")], sem=VE)
            pv["ctxA"] = e1
            pv["ctxB"] = e1
            e2 = sch.op("v", (lambda: (lambda v: v.tensor_copy(
                ctbf[:, :, :], ctf[:, :, :])))(),
                waits=[e1, pv.get("ctbf")], sem=VE)
            pv["ctf"] = e2
            ev_ln = {}
            for tt in range(4):
                bank = tt % 2
                evt = None
                for fj in range(8):
                    evt = sch.op("t", (lambda bank, fj, tt: (lambda t: t.transpose(
                        ps_tr[:, bank, fj * 128:(fj + 1) * 128],
                        ctbf[:, fj, tt * 128:(tt + 1) * 128],
                        identb[:, :])))(bank, fj, tt),
                        waits=[e2, ev_id, pv.get(("ps_tr", bank))],
                        sem=TE if fj == 7 else None)
                pv["ctbf"] = evt
                ea = sch.op("v", (lambda bank, tt: (lambda v: v.tensor_add(
                    x3b[:, tt, :], ps_tr[:, bank, :], x2cb[:, tt, :])))(bank, tt),
                    waits=[evt, ex2, pv.get(("x3b", tt))], sem=VE)
                pv[("ps_tr", bank)] = ea
                eln = emit_ln(sch, x3b[:, tt, :], x3b[:, tt, :], tmpf[:, :],
                              redb[:, 0:3], epsb[:, 0:1],
                              [ea, pv.get("tmpf"), ev_eps], VE, SE)
                pv["tmpf"] = eln
                ev_ln[tt] = eln
            pv["x2cb"] = ev_ln[3]
            ev_x3st[tc] = sch.dma("y",
                x3d.rearrange("(tc tt p) d -> tc p tt d", tt=4, p=128)[tc],
                x3b[:, :, :], waits=[ev_ln[t2] for t2 in range(4)], sem=X3ST)
            for tt in range(4):
                pv[("x3b", tt)] = ev_x3st[tc]
            for tt in range(4):
                i = tc * 4 + tt
                ec = sch.op("v", (lambda tt: (lambda v: v.tensor_copy(
                    x3c[:, :], x3b[:, tt, :])))(tt),
                    waits=[ev_ln[tt], pv.get("x3c")], sem=VE)
                evt = None
                for fj in range(8):
                    evt = sch.op("t", (lambda fj: (lambda t: t.transpose(
                        ps_tr[:, 0, fj * 128:(fj + 1) * 128],
                        x3c[:, fj * 128:(fj + 1) * 128],
                        identb[:, :])))(fj),
                        waits=[ec, pv.get(("ps_tr", 0))],
                        sem=TE if fj == 7 else None)
                pv["x3c"] = evt
                ee = sch.op("v", (lambda: (lambda v: v.tensor_copy(
                    x3tb[:, :, :],
                    ps_tr[:, 0, :].rearrange("p (f q) -> p f q", q=128))))(),
                    waits=[evt, pv.get("x3tb")], sem=VE)
                pv[("ps_tr", 0)] = ee
                ev_xtst[i] = sch.dma("y",
                    x3Td.rearrange("(j p) c -> p j c", p=128)[:, :, i * 128:(i + 1) * 128],
                    x3tb[:, :, :], waits=[ee], sem=XTST)
                pv["x3tb"] = ev_xtst[i]
                evg = None
                for dj in range(8):
                    evg = sch.op("t", (lambda dj: (lambda t: t.matmul(
                        ps_g[:, 0:16], x3tb[:, dj, :], wgb[:, dj, :],
                        start=(dj == 0), stop=(dj == 7))))(dj),
                        waits=[ee, ev_wg, pv.get("ps_g")], sem=TE if dj == 7 else None)
                es = sch.op("s", (lambda i: (lambda s: s.activation(
                    gsc[:, i, :], ps_g[:, 0:16], AF.Sigmoid)))(i),
                    waits=[evg, pv.get(("gsc", i))], sem=SE)
                pv["ps_g"] = es
                gv = gsc[:, i, :].rearrange("p (g e) -> p g e", e=4)
                pairs = [(0, 1), (0, 2), (0, 3), (1, 2), (1, 3), (2, 3)]
                epv = None
                for pi, (a, b) in enumerate(pairs):
                    dst = g2b[:, 4:8] if pi else g2b[:, 4:8]
                    if pi == 0:
                        epv = sch.op("v", (lambda gv, a, b: (lambda v: v.tensor_add(
                            g2b[:, 0:4], gv[:, :, a], gv[:, :, b])))(gv, a, b),
                            waits=[es, pv.get("g2b")], sem=VE)
                    else:
                        e_ = sch.op("v", (lambda gv, a, b: (lambda v: v.tensor_add(
                            g2b[:, 4:8], gv[:, :, a], gv[:, :, b])))(gv, a, b),
                            waits=[es, epv], sem=VE)
                        epv = sch.op("v", (lambda: (lambda v: v.tensor_max(
                            g2b[:, 0:4], g2b[:, 0:4], g2b[:, 4:8])))(),
                            waits=[e_], sem=VE)
                em1 = sch.op("v", (lambda: (lambda v: v.reduce_max(
                    redb[:, 3:4], g2b[:, 0:4], axis=AX.X)))(),
                    waits=[epv], sem=VE)
                et1 = sch.op("v", (lambda: (lambda v: v.tensor_scalar(
                    g2b[:, 4:8], g2b[:, 0:4], redb[:, 3:4], None, OP.is_ge)))(),
                    waits=[em1], sem=VE)
                et2 = sch.op("v", (lambda: (lambda v: v.scalar_tensor_tensor(
                    g2b[:, 4:8], g2b[:, 4:8], -BIG, g2b[:, 0:4],
                    OP.mult, OP.add)))(), waits=[et1], sem=VE)
                em2 = sch.op("v", (lambda: (lambda v: v.reduce_max(
                    redb[:, 4:5], g2b[:, 4:8], axis=AX.X)))(),
                    waits=[et2], sem=VE)
                egm = sch.op("v", (lambda: (lambda v: v.tensor_scalar(
                    g2b[:, 0:4], g2b[:, 0:4], redb[:, 4:5], None, OP.is_ge)))(),
                    waits=[em2], sem=VE)
                eal = None
                for g in range(4):
                    eal = sch.op("v", (lambda g: (lambda v: v.tensor_scalar(
                        gtmp[:, g * 4:(g + 1) * 4], onesb[:, 0:4],
                        g2b[:, g:g + 1], None, OP.mult)))(g),
                        waits=[egm, ev_on, pv.get("gtmp")], sem=VE)
                ems = sch.op("v", (lambda: (lambda v: v.tensor_scalar(
                    gtmp[:, :], gtmp[:, :], BIG, -BIG, OP.mult, OP.add)))(),
                    waits=[eal], sem=VE)
                ems2 = sch.op("v", (lambda i: (lambda v: v.tensor_add(
                    comb[:, i, :], gsc[:, i, :], gtmp[:, :])))(i),
                    waits=[ems], sem=VE)
                ecur = ems2
                for r in range(3):
                    em = sch.op("v", (lambda i: (lambda v: v.reduce_max(
                        redb[:, 5:6], comb[:, i, :], axis=AX.X)))(i),
                        waits=[ecur], sem=VE)
                    esl = sch.op("v", (lambda i: (lambda v: v.tensor_scalar(
                        gtmp[:, :], comb[:, i, :], redb[:, 5:6], None,
                        OP.is_ge)))(i), waits=[em], sem=VE)
                    ecur = sch.op("v", (lambda i: (lambda v: v.scalar_tensor_tensor(
                        comb[:, i, :], gtmp[:, :], -BIG, comb[:, i, :],
                        OP.mult, OP.add)))(i), waits=[esl], sem=VE)
                em4 = sch.op("v", (lambda i: (lambda v: v.reduce_max(
                    redb[:, 5:6], comb[:, i, :], axis=AX.X)))(i),
                    waits=[ecur], sem=VE)
                eal2 = None
                for g in range(4):
                    eal2 = sch.op("v", (lambda g: (lambda v: v.tensor_scalar(
                        gtmp[:, g * 4:(g + 1) * 4], onesb[:, 0:4],
                        g2b[:, g:g + 1], None, OP.mult)))(g),
                        waits=[em4], sem=VE)
                ems3 = sch.op("v", (lambda: (lambda v: v.tensor_scalar(
                    gtmp[:, :], gtmp[:, :], BIG, -BIG, OP.mult, OP.add)))(),
                    waits=[eal2], sem=VE)
                ems4 = sch.op("v", (lambda i: (lambda v: v.tensor_add(
                    gtmp[:, :], gsc[:, i, :], gtmp[:, :])))(i),
                    waits=[ems3], sem=VE)
                esel = sch.op("v", (lambda i: (lambda v: v.tensor_scalar(
                    gtmp[:, :], gtmp[:, :], redb[:, 5:6], None, OP.is_ge)))(i),
                    waits=[ems4], sem=VE)
                ew = sch.op("v", (lambda i: (lambda v: v.tensor_mul(
                    comb[:, i, :], gsc[:, i, :], gtmp[:, :])))(i),
                    waits=[esel], sem=VE)
                pv[("gsc", i)] = ew
                pv["gtmp"] = ew
                ewsum = sch.op("v", (lambda i: (lambda v: v.reduce_sum(
                    redb[:, 5:6], comb[:, i, :], axis=AX.X)))(i),
                    waits=[ew], sem=VE)
                ews2 = sch.op("v", (lambda: (lambda v: v.tensor_scalar(
                    redb[:, 5:6], redb[:, 5:6], 1e-6, None, OP.add)))(),
                    waits=[ewsum], sem=VE)
                ewi = sch.op("v", (lambda: (lambda v: v.reciprocal(
                    redb[:, 6:7], redb[:, 5:6])))(), waits=[ews2], sem=VE)
                ecmb = sch.op("v", (lambda i: (lambda v: v.tensor_scalar(
                    comb[:, i, :], comb[:, i, :], redb[:, 6:7], None,
                    OP.mult)))(i), waits=[ewi], sem=VE)
                pv[("comb", i)] = ecmb
                pv["g2b"] = ecmb

        def load_pair(eg, wait_ev):
            evw = None
            if eg < 8:
                for sl2 in range(2):
                    e = eg * 2 + sl2
                    for dj in range(8):
                        evw = sch.dma("g", w1b[:, sl2, dj, :], w1eT_r[e, dj],
                                      waits=[wait_ev], sem=WEL)
                        evw = sch.dma("g", w3b[:, sl2, dj, :], w3eT_r[e, dj],
                                      waits=[None], sem=WEL)
                    for fj in range(4):
                        evw = sch.dma("g", w2b[:, sl2, fj, :], w2eT_r[e, fj],
                                      waits=[None], sem=WEL)
            else:
                for sl2 in range(2):
                    for dj in range(8):
                        evw = sch.dma("g", w1b[:, sl2, dj, :],
                                      w1sT_r[dj, :, sl2 * 512:(sl2 + 1) * 512],
                                      waits=[wait_ev], sem=WEL)
                        evw = sch.dma("g", w3b[:, sl2, dj, :],
                                      w3sT_r[dj, :, sl2 * 512:(sl2 + 1) * 512],
                                      waits=[None], sem=WEL)
                    for fj in range(4):
                        evw = sch.dma("g", w2b[:, sl2, fj, :],
                                      w2sT_r[sl2 * 4 + fj], waits=[None], sem=WEL)
            return evw

        ev_w = load_pair(0, ev_xtst[NT - 1])
        ev_xt = {}

        def load_xt(i, wait_ev):
            sl3 = i % 2
            ev_xt[i] = sch.dma("g", xtcb[:, sl3, :, :],
                x3Td.rearrange("(j p) c -> p j c", p=128)[:, :, i * 128:(i + 1) * 128],
                waits=[wait_ev, ev_xtst[i]], sem=XTL[sl3])

        last_w_use = None
        ev_ya = {}
        for eg in range(9):
            if eg == 0:
                load_xt(0, None)
                load_xt(1, None)
            for i in range(NT):
                sl3 = i % 2
                evy = None
                for sl2 in range(2):
                    e = eg * 2 + sl2
                    bank = mmc % 3
                    ev = None
                    for dj in range(8):
                        ev = sch.op("t", (lambda bank, dj, sl2, sl3: (lambda t: t.matmul(
                            ps_mm[:, bank, :], xtcb[:, sl3, dj, :],
                            w1b[:, sl2, dj, :],
                            start=(dj == 0), stop=(dj == 7))))(bank, dj, sl2, sl3),
                            waits=[ev_w, ev_xt[i], mm_evac.get(mmc - 3)],
                            sem=TE if dj == 7 else None)
                    eg1 = sch.op("s", (lambda bank: (lambda s: s.activation(
                        gsb[:, :], ps_mm[:, bank, :], AF.Silu)))(bank),
                        waits=[ev, pv.get("gsb")], sem=SE)
                    mm_evac[mmc] = eg1
                    mmc += 1
                    bank = mmc % 3
                    ev = None
                    for dj in range(8):
                        ev = sch.op("t", (lambda bank, dj, sl2, sl3: (lambda t: t.matmul(
                            ps_mm[:, bank, :], xtcb[:, sl3, dj, :],
                            w3b[:, sl2, dj, :],
                            start=(dj == 0), stop=(dj == 7))))(bank, dj, sl2, sl3),
                            waits=[mm_evac.get(mmc - 3)], sem=TE if dj == 7 else None)
                    last_w_use = ev
                    if eg < 8:
                        egu = sch.op("v", (lambda bank, i, e: (lambda v: v.scalar_tensor_tensor(
                            gub[:, :], ps_mm[:, bank, :], comb[:, i, e:e + 1],
                            gsb[:, :], OP.mult, OP.mult)))(bank, i, e),
                            waits=[ev, eg1, pv.get(("comb", i)),
                                   pv.get("gub")], sem=VE)
                    else:
                        egu = sch.op("v", (lambda bank: (lambda v: v.tensor_mul(
                            gub[:, :], ps_mm[:, bank, :], gsb[:, :])))(bank),
                            waits=[ev, eg1, pv.get("gub")], sem=VE)
                    pv["gsb"] = egu
                    mm_evac[mmc] = egu
                    mmc += 1
                    evt = None
                    for f4 in range(4):
                        evt = sch.op("t", (lambda f4: (lambda t: t.transpose(
                            ps_tr[:, 0, f4 * 128:(f4 + 1) * 128],
                            gub[:, f4 * 128:(f4 + 1) * 128],
                            identb[:, :])))(f4),
                            waits=[egu, pv.get(("ps_tr", 0))],
                            sem=TE if f4 == 3 else None)
                    pv["gub"] = evt
                    egt = sch.op("v", (lambda: (lambda v: v.tensor_copy(
                        guT[:, :, :],
                        ps_tr[:, 0, 0:512].rearrange("p (f q) -> p f q", q=128))))(),
                        waits=[evt, pv.get("guT")], sem=VE)
                    pv[("ps_tr", 0)] = egt
                    for dch in range(2):
                        ev = None
                        for f4 in range(4):
                            ev = sch.op("t", (lambda dch, f4, sl2: (lambda t: t.matmul(
                                ps_y[:, dch, :], guT[:, f4, :],
                                w2b[:, sl2, f4, dch * 512:(dch + 1) * 512],
                                start=(sl2 == 0 and f4 == 0),
                                stop=(sl2 == 1 and f4 == 3))))(dch, f4, sl2),
                                waits=[egt, pv.get(("ps_y", dch))],
                                sem=TE if f4 == 3 else None)
                        evy = ev
                        if sl2 == 1:
                            pv[("ylast", dch)] = ev
                    pv["guT"] = evy
                eyv = []
                for dch in range(2):
                    eya = sch.op("v", (lambda dch: (lambda v: v.tensor_copy(
                        y1a[:, dch * 512:(dch + 1) * 512],
                        ps_y[:, dch, :])))(dch),
                        waits=[pv[("ylast", dch)], ev_ya.get((eg - 1, i)),
                               ev_ya.get((eg, i - 1))], sem=VE)
                    pv[("ps_y", dch)] = eya
                    eyv.append(eya)
                kw = {} if eg == 0 else {"accum_op": OP.add}
                ev_ya[(eg, i)] = sch.dma("g", y1d[i * 128:(i + 1) * 128, :],
                                         y1a[:, :], waits=eyv,
                                         sem=YDST[i % 2], **kw)
                nxt = i + 2
                if nxt < NT:
                    load_xt(nxt, last_w_use)
                elif eg < 8:
                    load_xt(nxt - NT, last_w_use)
            if eg < 8:
                ev_w = load_pair(eg + 1, last_w_use)

        for i in range(NT):
            exl = sch.dma("y", tmpf[:, :], x3d[i * 128:(i + 1) * 128, :],
                          waits=[ev_x3st[i // 4], pv.get("tmpf")], sem=X3L)
            eyl = sch.dma("g", y1a[:, :], y1d[i * 128:(i + 1) * 128, :],
                          waits=[ev_ya[(8, i)], pv.get("y1aC")], sem=YDL)
            e1 = sch.op("v", (lambda i: (lambda v: v.tensor_add(
                x3b[:, 0, :], tmpf[:, :], y1a[:, :])))(i),
                waits=[exl, eyl, pv.get(("x3b", 0))], sem=VE)
            pv["y1aC"] = e1
            eln = emit_ln(sch, x3b[:, 0, :], x3b[:, 0, :], tmpf[:, :],
                          redb[:, 0:3], epsb[:, 0:1],
                          [e1], VE, SE)
            pv["tmpf"] = eln
            est = sch.dma("y", outo[i * 128:(i + 1) * 128, :],
                          x3b[:, 0, :], waits=[eln], sem=OST)
            pv[("x3b", 0)] = est

        sch.op("y", lambda y: y.nop(), waits=[(OST, OST.n)])
        with nc.Block() as blk:
            sch.finalize(blk)
    return nc
